# version 09
# baseline (speedup 1.0000x reference)
import dataclasses
from contextlib import ExitStack

import numpy as np

import concourse.bass as bass
import concourse.mybir as mybir
from concourse.bass_utils import run_bass_kernel_spmd

F32 = mybir.dt.float32
BF16 = mybir.dt.bfloat16
F16 = mybir.dt.float16
I8 = mybir.dt.int8
AF = mybir.ActivationFunctionType
ALU = mybir.AluOpType
AX = mybir.AxisListType

N_CORES = 8


@dataclasses.dataclass
class Cfg:
    rows: int = 2048
    H: int = 4096
    I: int = 11008
    O: int = 4096
    n_iter: int = 13
    eps: float = 1e-5

    @property
    def HC(self):  return self.H // 128
    @property
    def IT(self):  return self.I // 128
    @property
    def RT(self):  return self.rows // 128
    @property
    def RTH(self): return self.RT // 2
    @property
    def RH(self):  return self.rows // 2
    @property
    def NBW(self): return min(512, self.H)
    @property
    def NB(self):  return self.H // self.NBW
    @property
    def OH(self):  return self.O // 2
    @property
    def k(self):   return self.I // 2
    @property
    def IV(self):  return min((self.IT * 60 // 128) * 128, self.I - 128) if self.I > 1024 else (self.IT // 2) * 128
    @property
    def IS(self):  return self.I - self.IV


def _chunks(n, c=512):
    return [(i, min(i + c, n)) for i in range(0, n, c)]


class Sem:

    def __init__(self, st, nc, name):
        self.h = st.enter_context(nc.semaphore(name))
        self.n = 0

    def inc(self, inst, k=1):
        inst.then_inc(self.h, k)
        self.n += k
        return self.n


class Stream:

    def __init__(self, st, nc, name):
        self.h = st.enter_context(nc.semaphore(name))
        self.n = 0

    def begin(self, e):
        if self.n:
            e.wait_ge(self.h, self.n)

    def dma(self, e, out, in_, transpose=False):
        if transpose:
            inst = e.dma_start(out=out, in_=in_, transpose=True)
        else:
            inst = e.dma_start(out=out, in_=in_)
        inst.then_inc(self.h, 16)
        self.n += 16
        return self.n

    def wait(self, e, tgt=None):
        t = self.n if tgt is None else tgt
        if t:
            e.wait_ge(self.h, t)


class Builder:
    def __init__(self, cfg: Cfg):
        self.cfg = cfg
        self.nc = bass.Bass("TRN2", target_bir_lowering=False)

    def T(self, f):  self.block.tensor(f)

    def V(self, f):
        def g(v):
            if self.ve.n:
                v.wait_ge(self.ve.h, self.ve.n)
            f(v)
        self.block.vector(g)

    def S(self, f):
        def g(s):
            if self.sc.n:
                s.wait_ge(self.sc.h, self.sc.n)
            f(s)
        self.block.scalar(g)

    def G(self, f):  self.block.gpsimd(f)
    def Y(self, f):  self.block.sync(f)

    def build(self):
        nc, cfg = self.nc, self.cfg
        self.x_t = nc.dram_tensor("x_t", [2, 128, cfg.HC * cfg.RH], F32, kind="ExternalInput").ap()
        self.x_row = nc.dram_tensor("x_row", [cfg.rows, cfg.H], F32, kind="ExternalInput").ap()
        self.wg_t = nc.dram_tensor("wg_t", [cfg.IT, 128, cfg.HC * 128], F32, kind="ExternalInput").ap()
        self.wu_t = nc.dram_tensor("wu_t", [cfg.IT, 128, cfg.HC * 128], F32, kind="ExternalInput").ap()
        self.wd_t = nc.dram_tensor("wd_t", [cfg.NB, 128, cfg.IT, cfg.NBW], F32, kind="ExternalInput").ap()
        self.wf_t = nc.dram_tensor("wf_t", [4, 128, cfg.HC * (cfg.O // 4)], I8, kind="ExternalInput").ap()
        self.scale_b = nc.dram_tensor("scale_b", [128, cfg.O], F32, kind="ExternalInput").ap()
        self.bias_b = nc.dram_tensor("bias_b", [128, cfg.O], F32, kind="ExternalInput").ap()
        self.out_d = nc.dram_tensor("out", [cfg.rows, cfg.H], F32, kind="ExternalOutput").ap()
        self.aT_d = nc.dram_tensor("aT_s", [cfg.IT, 128, cfg.rows], F16, kind="Internal").ap()
        self.pT_d = nc.dram_tensor("pT_s", [cfg.IT, 128, cfg.rows], BF16, kind="Internal").ap()
        self.pm_d = nc.dram_tensor("pm_s", [cfg.RT, 128, cfg.IT, 128], BF16, kind="Internal").ap()
        self.tflat_d = nc.dram_tensor("tflat_s", [1, cfg.rows], F32, kind="Internal").ap()
        self.C_d = nc.dram_tensor("C_s", [cfg.RT, 128, cfg.H], F32, kind="Internal").ap()
        self.x2_d = nc.dram_tensor("x2_s", [cfg.RT, 128, cfg.H], F32, kind="Internal").ap()
        self.x2b_d = nc.dram_tensor("x2b_s", [cfg.RT, cfg.HC, 128, 128], BF16, kind="Internal").ap()
        self.y_d = nc.dram_tensor("y_s", [cfg.RT, 128, cfg.O], F32, kind="Internal").ap()

        with ExitStack() as st:
            self.te = Sem(st, nc, "te")
            self.ve = Sem(st, nc, "ve")
            self.sc = Sem(st, nc, "sc")
            self.dout = Stream(st, nc, "dout")
            self.d_x = Stream(st, nc, "d_x")
            self.d_w = [Stream(st, nc, f"d_w{i}") for i in range(2)]
            self.d_wd = Stream(st, nc, "d_wd")
            self.d_wf = Stream(st, nc, "d_wf")
            self.d_as = [Stream(st, nc, f"d_as{i}") for i in range(2)]
            self.d_ps = [Stream(st, nc, f"d_ps{i}") for i in range(2)]
            self.d_ar = [Stream(st, nc, f"d_ar{i}") for i in range(2)]
            self.d_hi = Stream(st, nc, "d_hi")
            self.d_tb = Stream(st, nc, "d_tb")
            self.d_mk = [Stream(st, nc, f"d_mk{i}") for i in range(3)]
            self.d_pmc = [Stream(st, nc, f"d_pmc{i}") for i in range(3)]
            self.d_pm = [Stream(st, nc, f"d_pm{i}") for i in range(2)]
            self.d_cs = [Stream(st, nc, f"d_cs{i}") for i in range(2)]
            self.d_cx = Stream(st, nc, "d_cx")
            self.d_x2 = Stream(st, nc, "d_x2")
            self.d_sb = Stream(st, nc, "d_sb")
            self.d_xbt = [Stream(st, nc, f"d_xbt{i}") for i in range(2)]
            self.d_ys = [Stream(st, nc, f"d_ys{i}") for i in range(2)]
            self.d_yx2 = Stream(st, nc, "d_yx2")
            self.all_sems = ([self.te, self.ve, self.sc, self.dout, self.d_x,
                              self.d_wd, self.d_wf, self.d_hi, self.d_tb,
                              self.d_cx, self.d_x2, self.d_sb, self.d_yx2]
                             + self.d_w + self.d_as + self.d_ps + self.d_ar
                             + self.d_mk + self.d_pmc + self.d_pm + self.d_cs
                             + self.d_xbt + self.d_ys)

            with nc.Block() as block:
                self.block = block
                self._emit_all(st)
        return nc

    def _emit_all(self, st):
        nc, cfg = self.nc, self.cfg
        with ExitStack() as p1s:
            sb = lambda name, shape, dt: p1s.enter_context(nc.sbuf_tensor(name, shape, dt))
            self.xT = sb("xT", [128, cfg.HC * cfg.RH], BF16)
            self.wgb = [sb(f"wgb{i}", [128, cfg.HC * 128], BF16) for i in range(2)]
            self.wub = [sb(f"wub{i}", [128, cfg.HC * 128], BF16) for i in range(2)]
            self.s_sb = [sb(f"s_sb{i}", [128, cfg.RH], F32) for i in range(2)]
            self.a_sb = [sb(f"a_sb{i}", [128, cfg.RH], F16) for i in range(2)]
            self.p_sb = [sb(f"p_sb{i}", [128, cfg.RH], BF16) for i in range(2)]
            self.a_row = [sb(f"a_row{i}", [128, cfg.I], F16) for i in range(2)]
            self.dump = sb("dump", [128, cfg.I], F16)
            self.lo = [[sb(f"lo{l}{i}", [128, 1], F32) for i in range(2)] for l in range(2)]
            self.hi = [[sb(f"hi{l}{i}", [128, 1], F32) for i in range(2)] for l in range(2)]
            self.mid = [sb(f"mid{l}", [128, 1], F32) for l in range(2)]
            self.mid2 = [sb(f"mid2{l}", [128, 1], F32) for l in range(2)]
            self.cnt_v = [sb(f"cnt_v{l}", [128, 1], F32) for l in range(2)]
            self.S_s = [sb(f"S_s{l}", [128, 1], F32) for l in range(2)]
            self.cond = [sb(f"cond{l}", [128, 1], F32) for l in range(2)]
            self.ge_m = [sb(f"ge_m{l}", [128, 1], mybir.dt.uint32) for l in range(2)]
            self.t_bc = sb("t_bc", [128, cfg.RH], F32)
            self.t_bc16 = sb("t_bc16", [128, cfg.RH], F16)
            self.aT_c = [sb(f"aT_c{i}", [128, cfg.RH], F16) for i in range(3)]
            self.pT_c = [sb(f"pT_c{i}", [128, cfg.RH], BF16) for i in range(3)]
            self.m_c = [sb(f"m_c{i}", [128, cfg.RH], BF16) for i in range(3)]
            self.pm_c = [sb(f"pm_c{i}", [128, cfg.RH], BF16) for i in range(3)]
            with ExitStack() as ps1:
                pt = lambda name: ps1.enter_context(
                    nc.psum_tensor(name, [128, max(cfg.RH, 512)], F32))
                self.ps_g = [pt("ps_g0"), pt("ps_g1")]
                self.ps_u = [pt("ps_u0"), pt("ps_u1")]

                self.wg_ready = {}; self.wu_ready = {}; self.x_ready = {}
                self.g_done = {}; self.u_done = {}
                self.silu_done = {}; self.abs_done = {}; self.pmul_done = {}
                self.w_slot_free = [0, 0]
                self.bi_v = 0
                self.bi_lane_v = [0, 0]
                self.unit_mode = "split"
                self.pulled_cost = 0.0
                self.mid2_t = 0
                self.ss_t = 0
                self.cntv_t = 0
                self.a_ready = [0, 0]
                self.a_lastuse_v = [0, 0]
                self.a_lastuse_s = [0, 0]
                self.tb16_t = 0
                self.mask_lastuse = [0, 0, 0]
                self.as_fence = {}
                self.ps_fence = {}
                self.units = None

                with nc.named_scope("p1h0"):
                    self._emit_p1(0)
                self.pulled_cost = 0.0
                self.units = iter(list(self._p15_units(0)))
                with nc.named_scope("p1h1"):
                    self._emit_p1(1)
                    self.unit_mode = "split"
                    self._drain_units()
                with nc.named_scope("p15h1"):
                    self.units = iter(list(self._p15_units(1)))
                    self._drain_units()
        self._emit_join()

        with ExitStack() as p2s:
            sb = lambda name, shape, dt: p2s.enter_context(nc.sbuf_tensor(name, shape, dt))
            ITH = (cfg.IT + 1) // 2
            self.wd_sl = [sb(f"wd_sl{i}", [128, ITH * cfg.NBW], BF16) for i in range(3)]
            self.pm_sb = [sb(f"pm_sb{i}", [128, cfg.IT * 128], BF16) for i in range(2)]
            self.c_sb = [sb(f"c_sb{i}", [128, cfg.NBW], F32) for i in range(2)]
            with ExitStack() as ps2:
                self.ps_c = [ps2.enter_context(nc.psum_tensor(f"ps_c{i}", [128, 512], F32))
                             for i in range(2)]
                with nc.named_scope("p2"):
                    self._emit_p2()
        self._emit_join()

        with ExitStack() as p3as:
            sb = lambda name, shape, dt: p3as.enter_context(nc.sbuf_tensor(name, shape, dt))
            ln = [self._alloc_ln(sb, pfx=f"a{i}") for i in range(2)]
            cload = [sb(f"cload{i}", [128, cfg.H], F32) for i in range(2)]
            xload = [sb(f"xload{i}", [128, cfg.H], F32) for i in range(2)]
            v_sb = [sb(f"v_sb{i}", [128, cfg.H], F32) for i in range(2)]
            x2buf = [sb(f"x2buf{i}", [128, cfg.H], F32) for i in range(2)]
            x2bbuf = [sb(f"x2bbuf{i}", [128, cfg.H], BF16) for i in range(2)]
            with nc.named_scope("p3a"):
                self._emit_p3a(ln, cload, xload, v_sb, x2buf, x2bbuf)
        self._emit_join()

        with ExitStack() as p3bs:
            sb = lambda name, shape, dt: p3bs.enter_context(nc.sbuf_tensor(name, shape, dt))
            OQ = cfg.O // 4
            wf_sb = [sb(f"wf_sb{i}", [128, cfg.HC * OQ], BF16) for i in range(2)]
            x2bT = [sb(f"x2bT{i}", [128, cfg.HC * 128], BF16) for i in range(2)]
            scale_sb = sb("scale_sb", [128, cfg.O], F32)
            bias_sb = sb("bias_sb", [128, cfg.O], F32)
            y_sb = [sb(f"y_sb{i}", [128, OQ], F32) for i in range(2)]
            with ExitStack() as ps3:
                ps_y = [ps3.enter_context(nc.psum_tensor(f"ps_y{i}", [128, max(OQ, 512)], F32))
                        for i in range(2)]
                with nc.named_scope("p3b"):
                    self._emit_p3b(wf_sb, x2bT, scale_sb, bias_sb, y_sb, ps_y)
        self._emit_join()

        with ExitStack() as p3cs:
            sb = lambda name, shape, dt: p3cs.enter_context(nc.sbuf_tensor(name, shape, dt))
            ln = [self._alloc_ln(sb, pfx=f"c{i}") for i in range(2)]
            yload = [sb(f"yload{i}", [128, cfg.O], F32) for i in range(2)]
            x2load = [sb(f"x2load{i}", [128, cfg.H], F32) for i in range(2)]
            v_sb = [sb(f"v_sbc{i}", [128, cfg.H], F32) for i in range(2)]
            o_sb = [sb(f"o_sb{i}", [128, cfg.H], F32) for i in range(2)]
            with nc.named_scope("p3c"):
                self._emit_p3c(ln, yload, x2load, v_sb, o_sb)

        total = self.dout.n
        self.Y(lambda s: s.wait_ge(self.dout.h, total))

    def _emit_join(self):
        tgts = [(s.h, s.n) for s in self.all_sems if s.n > 0]
        for emit in (self.T, self.V, self.S, self.G, self.Y):
            def f(e, tgts=tgts):
                for h, n in tgts:
                    e.wait_ge(h, n)
            emit(f)

    def _emit_p1(self, half):
        cfg = self.cfg
        hs = slice(half * cfg.RH, (half + 1) * cfg.RH)

        def gx(g, half=half):
            self.d_x.begin(g)
            if half == 1:
                g.wait_ge(self.te.h, self.u_done[(0, cfg.IT - 1)])
            src = self.x_t[half]
            n = cfg.HC * cfg.RH
            ngr = 4 if n % 4 == 0 else 1
            step = n // ngr
            for j in range(ngr):
                self.d_x.dma(g, self.xT[:, j*step:(j+1)*step], src[:, j*step:(j+1)*step])
            self.x_ready[half] = self.d_x.n
        self.G(gx)

        for it in range(cfg.IT):
            slot = (half * cfg.IT + it) % 2
            pset = it % 2

            def gw(g, it=it, slot=slot, half=half):
                self.d_w[slot].begin(g)
                if self.w_slot_free[slot] > 0:
                    g.wait_ge(self.te.h, self.w_slot_free[slot])
                self.d_w[slot].dma(g, self.wgb[slot][:], self.wg_t[it])
                self.wu_ready[(half, it)] = self.d_w[slot].dma(g, self.wub[slot][:], self.wu_t[it])
            self.G(gw)

            def tm(t, it=it, slot=slot, pset=pset, half=half):
                t.wait_ge(self.d_w[slot].h, self.wu_ready[(half, it)])
                t.wait_ge(self.d_x.h, self.x_ready[half])
                prev = (half, it - 2) if it >= 2 else (half - 1, cfg.IT - 2 + it) if half == 1 else None
                if prev is not None and prev in self.silu_done:
                    t.wait_ge(self.sc.h, self.silu_done[prev])
                    t.wait_ge(self.ve.h, self.pmul_done[prev])
                ck = _chunks(cfg.RH)
                for mat, wsb, ps in ((0, self.wgb[slot], self.ps_g[pset]),
                                     (1, self.wub[slot], self.ps_u[pset])):
                    last = None
                    for hc in range(cfg.HC):
                        for (c0, c1) in ck:
                            last = t.matmul(
                                ps[:, c0:c1],
                                wsb[:, hc * 128:(hc + 1) * 128],
                                self.xT[:, hc * cfg.RH + c0: hc * cfg.RH + c1],
                                start=(hc == 0), stop=(hc == cfg.HC - 1))
                    tgt = self.te.inc(last, 1)
                    if mat == 0:
                        self.g_done[(half, it)] = tgt
                    else:
                        self.u_done[(half, it)] = tgt
                self.w_slot_free[slot] = self.u_done[(half, it)]
            self.T(tm)

            def sca(s, it=it, pset=pset, half=half):
                s.wait_ge(self.te.h, self.g_done[(half, it)])
                prev = (half, it - 2) if it >= 2 else (half - 1, cfg.IT - 2 + it) if half == 1 else None
                if prev is not None and prev in self.abs_done:
                    s.wait_ge(self.ve.h, self.abs_done[prev])
                self.silu_done[(half, it)] = self.sc.inc(
                    s.activation(out=self.s_sb[pset][:],
                                 in_=self.ps_g[pset][:, 0:cfg.RH],
                                 func=AF.Silu), 1)
            self.S(sca)

            def vp(v, it=it, pset=pset, half=half):
                v.wait_ge(self.te.h, self.u_done[(half, it)])
                v.wait_ge(self.sc.h, self.silu_done[(half, it)])
                self.d_ps[pset].wait(v)
                self.pmul_done[(half, it)] = self.ve.inc(
                    v.tensor_tensor(out=self.p_sb[pset][:], in0=self.s_sb[pset][:],
                                    in1=self.ps_u[pset][:, 0:cfg.RH], op=ALU.mult), 1)
                self.d_as[pset].wait(v)
                self.abs_done[(half, it)] = self.ve.inc(
                    v.tensor_tensor(out=self.a_sb[pset][:], in0=self.s_sb[pset][:],
                                    in1=self.s_sb[pset][:], op=ALU.mult), 1)
            self.V(vp)

            def ys(s, it=it, pset=pset, half=half):
                self.d_as[pset].begin(s)
                s.wait_ge(self.ve.h, self.abs_done[(half, it)])
                self.d_as[pset].dma(s, self.aT_d[it, :, hs], self.a_sb[pset][:])
                self.d_ps[pset].begin(s)
                s.wait_ge(self.ve.h, self.pmul_done[(half, it)])
                self.d_ps[pset].dma(s, self.pT_d[it, :, hs], self.p_sb[pset][:])
            self.Y(ys)

            self._pull_units_budget((it + 1) * 16.0)
        self.as_fence[half] = (self.d_as[0].n, self.d_as[1].n)
        self.ps_fence[half] = (self.d_ps[0].n, self.d_ps[1].n)

    def _pull_units_budget(self, budget):
        if self.units is None:
            return
        while self.pulled_cost < budget:
            u = next(self.units, None)
            if u is None:
                self.units = None
                return
            c, f = u
            f()
            self.pulled_cost += c

    def _drain_units(self):
        if self.units is None:
            return
        for u in self.units:
            u[1]()
        self.units = None

    def _p15_units(self, half):
        cfg = self.cfg
        for rp in range(0, cfg.RTH, 2):
            pair = [half * cfg.RTH + rp]
            if rp + 1 < cfg.RTH:
                pair.append(half * cfg.RTH + rp + 1)
            for rt in pair:
                yield (12.0, lambda rt=rt: self._u_binit(rt))
            for j in range(cfg.n_iter):
                yield (17.0, lambda pair=tuple(pair), j=j: self._u_biter_pair(pair, j))
            for rt in pair:
                yield (2.0, lambda rt=rt: self._u_bfin(rt))
        yield (3.0, lambda half=half: self._u_tbcast(half))
        for it in range(cfg.IT):
            yield (3.0, lambda it=it: self._u_mask(half, it))

    def _u_binit(self, rt):
        cfg = self.cfg
        ab = rt % 2
        half = rt // cfg.RTH

        def y(s, rt=rt, ab=ab, half=half):
            self.d_ar[ab].begin(s)
            for i in (0, 1):
                s.wait_ge(self.d_as[i].h, self.as_fence[half][i])
            if self.a_lastuse_v[ab]:
                s.wait_ge(self.ve.h, self.a_lastuse_v[ab])
            if self.a_lastuse_s[ab]:
                s.wait_ge(self.sc.h, self.a_lastuse_s[ab])
            src = self.aT_d.rearrange("it p r -> (it p) r")[:, rt * 128:(rt + 1) * 128]
            self.a_ready[ab] = self.d_ar[ab].dma(s, self.a_row[ab][:], src, transpose=True)
        self.Y(y)

        def v(v, rt=rt, ab=ab):
            v.wait_ge(self.d_ar[ab].h, self.a_ready[ab])
            self.d_hi.wait(v)
            if self.bi_lane_v[ab]:
                v.wait_ge(self.ve.h, self.bi_lane_v[ab])
            self.ve.inc(v.reduce_max(out=self.hi[ab][0][:], in_=self.a_row[ab][:], axis=AX.X), 1)
            self.ve.inc(v.memset(self.lo[ab][0][:], 0.0), 1)
            self.bi_lane_v[ab] = self.ve.n
        self.V(v)

    def _u_biter_pair(self, pair, j):
        cfg = self.cfg
        p, q = j % 2, 1 - (j % 2)
        vonly = self.unit_mode == "vonly"
        nv = cfg.I if vonly else cfg.IV
        w = (2.0 ** -9, 2.0 ** -6, 2.0 ** -3)[j] if j < 3 else 0.5
        mid2_t = {}
        cntv_t = {}
        ss_t = {}

        def v1(v):
            for rt in pair:
                L = rt % 2
                v.wait_ge(self.ve.h, self.bi_lane_v[L])
                t = self.ve.inc(v.tensor_sub(self.mid[L][:], self.hi[L][p][:], self.lo[L][p][:]), 1)
                v.wait_ge(self.ve.h, t)
                mid2_t[L] = self.ve.inc(
                    v.scalar_tensor_tensor(out=self.mid2[L][:], in0=self.mid[L][:],
                                           scalar=w, in1=self.lo[L][p][:],
                                           op0=ALU.mult, op1=ALU.add), 1)
            for rt in pair:
                L = rt % 2
                v.wait_ge(self.ve.h, mid2_t[L])
                if cntv_t:
                    v.wait_ge(self.ve.h, max(cntv_t.values()))
                cntv_t[L] = self.ve.inc(
                    v.tensor_scalar(out=self.dump[:, 0:nv], in0=self.a_row[L][:, 0:nv],
                                    scalar1=self.mid2[L][:], scalar2=None,
                                    op0=ALU.is_le, op1=ALU.add,
                                    accum_out=self.cnt_v[L][:]), 1)
                self.a_lastuse_v[L] = cntv_t[L]
        self.V(v1)

        if not vonly:
            def s1(s):
                for rt in pair:
                    L = rt % 2
                    s.wait_ge(self.ve.h, mid2_t[L])
                    if ss_t:
                        s.wait_ge(self.sc.h, max(ss_t.values()))
                    ss_t[L] = self.sc.inc(
                        s.activation(out=self.dump[:, cfg.IV:cfg.I],
                                     in_=self.a_row[L][:, cfg.IV:cfg.I],
                                     func=AF.Sign, scale=-1.0, bias=self.mid2[L][:],
                                     accum_out=self.S_s[L][:]), 1)
                    self.a_lastuse_s[L] = ss_t[L]
            self.S(s1)

        def v2(v):
            for rt in pair:
                L = rt % 2
                if vonly:
                    v.wait_ge(self.ve.h, cntv_t[L])
                    t = self.ve.inc(v.tensor_single_scalar(
                        out=self.ge_m[L][:], in_=self.cnt_v[L][:],
                        scalar=float(cfg.k), op=ALU.is_ge), 1)
                else:
                    v.wait_ge(self.sc.h, ss_t[L])
                    v.wait_ge(self.ve.h, cntv_t[L])
                    t = self.ve.inc(v.scalar_tensor_tensor(
                        out=self.cond[L][:], in0=self.S_s[L][:], scalar=0.5,
                        in1=self.cnt_v[L][:], op0=ALU.mult, op1=ALU.add), 1)
                    v.wait_ge(self.ve.h, t)
                    t = self.ve.inc(v.tensor_single_scalar(
                        out=self.ge_m[L][:], in_=self.cond[L][:],
                        scalar=float(cfg.k - cfg.IS / 2.0), op=ALU.is_ge), 1)
                v.wait_ge(self.ve.h, t)
                t = self.ve.inc(v.tensor_copy(out=self.hi[L][q][:], in_=self.hi[L][p][:]), 1)
                v.wait_ge(self.ve.h, t)
                t = self.ve.inc(v.copy_predicated(out=self.hi[L][q][:], mask=self.ge_m[L][:],
                                                  data=self.mid2[L][:]), 1)
                v.wait_ge(self.ve.h, t)
                t = self.ve.inc(v.tensor_copy(out=self.lo[L][q][:], in_=self.mid2[L][:]), 1)
                v.wait_ge(self.ve.h, t)
                t = self.ve.inc(v.copy_predicated(out=self.lo[L][q][:], mask=self.ge_m[L][:],
                                                  data=self.lo[L][p][:]), 1)
                self.bi_lane_v[L] = self.ve.n
        self.V(v2)

    def _u_bfin(self, rt):
        p = self.cfg.n_iter % 2
        L = rt % 2

        def y(s, rt=rt, p=p, L=L):
            self.d_hi.begin(s)
            s.wait_ge(self.ve.h, self.bi_lane_v[L])
            dst = self.tflat_d[0:1, rt * 128:(rt + 1) * 128].rearrange("one p -> p one")
            self.d_hi.dma(s, dst, self.hi[L][p][:])
        self.Y(y)

    def _u_tbcast(self, half):
        cfg = self.cfg

        def y(s, half=half):
            self.d_tb.begin(s)
            self.d_hi.wait(s)
            base = self.tflat_d[0:1, half * cfg.RH:(half + 1) * cfg.RH]
            src = dataclasses.replace(base, ap=[[0, 128]] + list(base.ap[1:]))
            if self.tb16_t:
                s.wait_ge(self.ve.h, self.tb16_t)
            self.d_tb.dma(s, self.t_bc[:], src)
        self.Y(y)

        def v(v):
            self.d_tb.wait(v)
            self.tb16_t = self.ve.inc(v.tensor_copy(out=self.t_bc16[:], in_=self.t_bc[:]), 1)
        self.V(v)

    def _u_mask(self, half, it):
        cfg = self.cfg
        mb = it % 3
        hs = slice(half * cfg.RH, (half + 1) * cfg.RH)

        def y(s, it=it, mb=mb, half=half):
            self.d_mk[mb].begin(s)
            for i in (0, 1):
                s.wait_ge(self.d_ps[i].h, self.ps_fence[half][i])
            if self.mask_lastuse[mb]:
                s.wait_ge(self.ve.h, self.mask_lastuse[mb])
            self.d_mk[mb].dma(s, self.aT_c[mb][:], self.aT_d[it, :, hs])
            self.mk_t = self.d_mk[mb].dma(s, self.pT_c[mb][:], self.pT_d[it, :, hs])
        self.Y(y)

        def v(v, it=it, mb=mb):
            v.wait_ge(self.d_mk[mb].h, self.mk_t)
            v.wait_ge(self.ve.h, self.tb16_t)
            self.d_pmc[mb].wait(v)
            t = self.ve.inc(v.tensor_tensor(out=self.m_c[mb][:], in0=self.aT_c[mb][:],
                                            in1=self.t_bc16[:], op=ALU.is_gt), 1)
            v.wait_ge(self.ve.h, t)
            self.pm_t = self.ve.inc(v.tensor_tensor(out=self.pm_c[mb][:], in0=self.pT_c[mb][:],
                                                    in1=self.m_c[mb][:], op=ALU.mult), 1)
            self.mask_lastuse[mb] = self.pm_t
        self.V(v)

        def y2(s, it=it, mb=mb, half=half):
            self.d_pmc[mb].begin(s)
            s.wait_ge(self.ve.h, self.pm_t)
            dst = self.pm_d[half * cfg.RTH:(half + 1) * cfg.RTH, :, it, :] \
                .rearrange("rt p c -> p rt c")
            src = self.pm_c[mb][:].rearrange("p (rt c) -> p rt c", rt=cfg.RTH)
            self.d_pmc[mb].dma(s, dst, src)
        self.block.scalar(y2)

    def _emit_p2(self):
        cfg = self.cfg
        ITH = (cfg.IT + 1) // 2
        nhp = 2 if cfg.IT > ITH else 1
        te0, ve0 = self.te.n, self.ve.n
        wd0 = self.d_wd.n
        pm0 = [s.n for s in self.d_pm]
        cs0 = [s.n for s in self.d_cs]

        NRT = cfg.RT

        def mm_done(nb, r):
            return te0 + nb * NRT + r + 1

        def evac_done(nb, r):
            return ve0 + nb * NRT + r + 1

        def pm_ready(nb, r):
            par = r % 2
            return pm0[par] + 16 * (nb * (NRT // 2) + r // 2 + 1)

        wd_seq = [(0, 0)]
        for nb in range(cfg.NB):
            if nhp > 1:
                wd_seq.append((nb, 1))
            if nb + 1 < cfg.NB:
                wd_seq.append((nb + 1, 0))
        wd_ready = {}
        for j, key in enumerate(wd_seq):
            wd_ready[key] = wd0 + 16 * (j + 1)

        def slot_of(nb, sh):
            return (nb * nhp + sh) % 3

        wd_src = self.wd_t.rearrange("nb p it h -> nb p (it h)")

        def g_all(g):
            done = set()
            def load_hp(nb, sh):
                sl = slot_of(nb, sh)
                self.d_wd.begin(g)
                idx = nb * nhp + sh
                if idx >= 3:
                    pnb = (idx - 3) // nhp
                    g.wait_ge(self.te.h, mm_done(pnb, NRT - 1))
                i0 = sh * ITH
                i1 = min(cfg.IT, i0 + ITH)
                cols = (i1 - i0) * cfg.NBW
                srcap = wd_src[nb][:, i0 * cfg.NBW: i0 * cfg.NBW + cols]
                t = self.d_wd.dma(g, self.wd_sl[sl][:, 0:cols], srcap)
                assert t == wd_ready[(nb, sh)], (t, wd_ready[(nb, sh)])
            for nb in range(cfg.NB):
                if nb == 0:
                    load_hp(0, 0)
                if nhp > 1:
                    load_hp(nb, 1)
                if nb + 1 < cfg.NB:
                    load_hp(nb + 1, 0)
        self.G(g_all)

        def y_all(s):
            for nb in range(cfg.NB):
                for r in range(NRT):
                    par = r % 2
                    self.d_pm[par].begin(s)
                    if nb * NRT + r >= 2:
                        pr = (nb * NRT + r) - 2
                        s.wait_ge(self.te.h, mm_done(pr // NRT, pr % NRT))
                    t = self.d_pm[par].dma(s, self.pm_sb[par][:],
                                           self.pm_d[r].rearrange("p it c -> p (it c)"))
                    assert t == pm_ready(nb, r)
        def y_all2(s):
            for nb in range(cfg.NB):
                for r in range(NRT):
                    par = r % 2
                    self.d_pm[par].begin(s)
                    if nb * NRT + r >= 2:
                        pr = (nb * NRT + r) - 2
                        s.wait_ge(self.te.h, mm_done(pr // NRT, pr % NRT))
                    t = self.d_pm[par].dma(s, self.pm_sb[par][:],
                                           self.pm_d[r].rearrange("p it c -> p (it c)"))
                    assert t == pm_ready(nb, r), (t, pm_ready(nb, r))
        self.Y(y_all2)

        def a_all(s):
            for nb in range(cfg.NB):
                for r in range(NRT):
                    par = r % 2
                    self.d_cs[par].begin(s)
                    s.wait_ge(self.ve.h, evac_done(nb, r))
                    self.d_cs[par].dma(s, self.C_d[r, :, nb * cfg.NBW:(nb + 1) * cfg.NBW],
                                       self.c_sb[par][:])
        self.block.scalar(a_all)

        def t_all(t):
            for nb in range(cfg.NB):
                for r in range(NRT):
                    par = r % 2
                    t.wait_ge(self.d_wd.h, wd_ready[(nb, 0)])
                    t.wait_ge(self.d_pm[par].h, pm_ready(nb, r))
                    idx = nb * NRT + r
                    if idx >= 2:
                        pr = idx - 2
                        t.wait_ge(self.ve.h, evac_done(pr // NRT, pr % NRT))
                    last = None
                    for it in range(cfg.IT):
                        if it == ITH and nhp > 1:
                            t.wait_ge(self.d_wd.h, wd_ready[(nb, 1)])
                        sl = slot_of(nb, it // ITH)
                        io = it % ITH
                        last = t.matmul(self.ps_c[par][:, 0:cfg.NBW],
                                        self.pm_sb[par][:, it * 128:(it + 1) * 128],
                                        self.wd_sl[sl][:, io * cfg.NBW:(io + 1) * cfg.NBW],
                                        start=(it == 0), stop=(it == cfg.IT - 1))
                    tgt = self.te.inc(last, 1)
                    assert tgt == mm_done(nb, r)
        self.T(t_all)

        def v_all(v):
            for nb in range(cfg.NB):
                for r in range(NRT):
                    par = r % 2
                    v.wait_ge(self.te.h, mm_done(nb, r))
                    idx = nb * NRT + r
                    if idx >= 2:
                        pr = idx - 2
                        pp = pr % 2
                        self.d_cs[pp].wait(v, cs0[pp] + 16 * (pr // 2 + 1))
                    tgt = self.ve.inc(
                        v.tensor_copy(out=self.c_sb[par][:], in_=self.ps_c[par][:, 0:cfg.NBW]), 1)
                    assert tgt == evac_done(nb, r)
        self.V(v_all)

    def _alloc_ln(self, sb, pfx=""):
        cfg = self.cfg
        return {
            "srow": sb(pfx + "srow", [128, 1], F32),
            "m": sb(pfx + "m", [128, 1], F32),
            "ssq": sb(pfx + "ssq", [128, 1], F32),
            "msq": sb(pfx + "msq", [128, 1], F32),
            "varb": sb(pfx + "varb", [128, 1], F32),
            "stdb": sb(pfx + "stdb", [128, 1], F32),
            "rstd": sb(pfx + "rstd", [128, 1], F32),
            "nmr": sb(pfx + "nmr", [128, 1], F32),
            "sqdump": sb(pfx + "sqdump", [128, cfg.H], F32),
        }

    def _emit_ln(self, ln, v_in, x2_out, ready_t):
        cfg = self.cfg
        res = {}

        def v1(v):
            v.wait_ge(self.ve.h, ready_t)
            t = self.ve.inc(v.reduce_sum(out=ln["srow"][:], in_=v_in[:], axis=AX.X), 1)
            t = self.ve.inc(v.scalar_tensor_tensor(
                out=ln["sqdump"][:], in0=v_in[:], scalar=1.0, in1=v_in[:],
                op0=ALU.mult, op1=ALU.mult, accum_out=ln["ssq"][:]), 1)
            v.wait_ge(self.ve.h, t)
            t = self.ve.inc(v.tensor_scalar_mul(ln["m"][:], ln["srow"][:], 1.0 / cfg.H), 1)
            v.wait_ge(self.ve.h, t)
            t = self.ve.inc(v.tensor_mul(ln["msq"][:], ln["m"][:], ln["m"][:]), 1)
            v.wait_ge(self.ve.h, t)
            t = self.ve.inc(v.tensor_scalar(out=ln["varb"][:], in0=ln["ssq"][:],
                                            scalar1=1.0 / cfg.H, scalar2=cfg.eps,
                                            op0=ALU.mult, op1=ALU.add), 1)
            v.wait_ge(self.ve.h, t)
            res["varb"] = self.ve.inc(
                v.tensor_sub(ln["varb"][:], ln["varb"][:], ln["msq"][:]), 1)
        self.V(v1)

        def s2(s):
            s.wait_ge(self.ve.h, res["varb"])
            res["stdb"] = self.sc.inc(
                s.activation(out=ln["stdb"][:], in_=ln["varb"][:], func=AF.Sqrt), 1)
        self.S(s2)

        def v3(v):
            v.wait_ge(self.sc.h, res["stdb"])
            t = self.ve.inc(v.reciprocal(out=ln["rstd"][:], in_=ln["stdb"][:]), 1)
            v.wait_ge(self.ve.h, t)
            t = self.ve.inc(v.scalar_tensor_tensor(
                out=ln["nmr"][:], in0=ln["m"][:], scalar=-1.0, in1=ln["rstd"][:],
                op0=ALU.mult, op1=ALU.mult), 1)
            v.wait_ge(self.ve.h, t)
            res["out"] = self.ve.inc(
                v.tensor_scalar(out=x2_out[:], in0=v_in[:],
                                scalar1=ln["rstd"][:], scalar2=ln["nmr"][:],
                                op0=ALU.mult, op1=ALU.add), 1)
        self.V(v3)
        return res["out"]

    def _ln_stats(self, v, ln, v_in):
        cfg = self.cfg
        t = self.ve.inc(v.reduce_sum(out=ln["srow"][:], in_=v_in[:], axis=AX.X), 1)
        t = self.ve.inc(v.scalar_tensor_tensor(
            out=ln["sqdump"][:], in0=v_in[:], scalar=1.0, in1=v_in[:],
            op0=ALU.mult, op1=ALU.mult, accum_out=ln["ssq"][:]), 1)
        v.wait_ge(self.ve.h, t)
        t = self.ve.inc(v.tensor_scalar_mul(ln["m"][:], ln["srow"][:], 1.0 / cfg.H), 1)
        v.wait_ge(self.ve.h, t)
        t = self.ve.inc(v.tensor_mul(ln["msq"][:], ln["m"][:], ln["m"][:]), 1)
        v.wait_ge(self.ve.h, t)
        t = self.ve.inc(v.tensor_scalar(out=ln["varb"][:], in0=ln["ssq"][:],
                                        scalar1=1.0 / cfg.H, scalar2=cfg.eps,
                                        op0=ALU.mult, op1=ALU.add), 1)
        v.wait_ge(self.ve.h, t)
        return self.ve.inc(v.tensor_sub(ln["varb"][:], ln["varb"][:], ln["msq"][:]), 1)

    def _ln_sqrt(self, s, ln, varb_t):
        s.wait_ge(self.ve.h, varb_t)
        return self.sc.inc(s.activation(out=ln["stdb"][:], in_=ln["varb"][:], func=AF.Sqrt), 1)

    def _ln_apply(self, v, ln, v_in, x2_out, stdb_t):
        v.wait_ge(self.sc.h, stdb_t)
        t = self.ve.inc(v.reciprocal(out=ln["rstd"][:], in_=ln["stdb"][:]), 1)
        v.wait_ge(self.ve.h, t)
        t = self.ve.inc(v.scalar_tensor_tensor(
            out=ln["nmr"][:], in0=ln["m"][:], scalar=-1.0, in1=ln["rstd"][:],
            op0=ALU.mult, op1=ALU.mult), 1)
        v.wait_ge(self.ve.h, t)
        return self.ve.inc(
            v.tensor_scalar(out=x2_out[:], in0=v_in[:],
                            scalar1=ln["rstd"][:], scalar2=ln["nmr"][:],
                            op0=ALU.mult, op1=ALU.add), 1)

    def _emit_p3a(self, ln, cload, xload, v_sb, x2buf, x2bbuf):
        cfg = self.cfg
        done = {}
        spills = {}
        for rp in range(0, cfg.RT, 2):
            pair = [rp] + ([rp + 1] if rp + 1 < cfg.RT else [])
            st = {}

            def y1(s, pair=pair):
                for rt in pair:
                    L = rt % 2
                    self.d_cx.begin(s)
                    if rt - 2 in done:
                        s.wait_ge(self.ve.h, done[rt - 2])
                    self.d_cx.dma(s, cload[L][:], self.C_d[rt])
                    st[(rt, "x")] = self.d_cx.dma(
                        s, xload[L][:], self.x_row[rt * 128:(rt + 1) * 128, :])
            self.Y(y1)

            def v1(v, pair=pair):
                for rt in pair:
                    L = rt % 2
                    v.wait_ge(self.d_cx.h, st[(rt, "x")])
                    if rt - 2 in done:
                        v.wait_ge(self.ve.h, done[rt - 2])
                    if rt - 2 in spills:
                        self.d_x2.wait(v, spills[rt - 2])
                    t = self.ve.inc(v.tensor_tensor(out=v_sb[L][:], in0=cload[L][:],
                                                    in1=xload[L][:], op=ALU.add), 1)
                    v.wait_ge(self.ve.h, t)
                    st[(rt, "var")] = self._ln_stats(v, ln[L], v_sb[L])
            self.V(v1)

            def s1(s, pair=pair):
                for rt in pair:
                    L = rt % 2
                    st[(rt, "std")] = self._ln_sqrt(s, ln[L], st[(rt, "var")])
            self.S(s1)

            def v2(v, pair=pair):
                for rt in pair:
                    L = rt % 2
                    t = self._ln_apply(v, ln[L], v_sb[L], x2buf[L], st[(rt, "std")])
                    v.wait_ge(self.ve.h, t)
                    done[rt] = self.ve.inc(v.tensor_copy(out=x2bbuf[L][:], in_=x2buf[L][:]), 1)
            self.V(v2)

            def y2(s, pair=pair):
                for rt in pair:
                    L = rt % 2
                    self.d_x2.begin(s)
                    s.wait_ge(self.ve.h, done[rt])
                    self.d_x2.dma(s, self.x2_d[rt], x2buf[L][:])
                    dst = self.x2b_d[rt].rearrange("dc r p -> r dc p")
                    srcap = x2bbuf[L][:].rearrange("r (dc p) -> r dc p", dc=cfg.HC)
                    spills[rt] = self.d_x2.dma(s, dst, srcap)
            self.block.scalar(y2)

    def _emit_p3b(self, wf_sb, x2bT, scale_sb, bias_sb, y_sb, ps_y):
        cfg = self.cfg
        OQ = cfg.O // 4
        st0 = {}

        def y0(s):
            self.d_sb.begin(s)
            st0["sc"] = self.d_sb.dma(s, scale_sb[:], self.scale_b[:])
            st0["bi"] = self.d_sb.dma(s, bias_sb[:], self.bias_b[:])
        self.Y(y0)

        wf_ready = {}
        mm_done = {}
        y_done = {}
        xbt_lastuse = [0, 0]
        wf_lastuse = [0, 0]

        def load_wf(g, oq):
            sl = oq % 2
            self.d_wf.begin(g)
            if wf_lastuse[sl]:
                g.wait_ge(self.te.h, wf_lastuse[sl])
            wf_ready[oq] = self.d_wf.dma(g, wf_sb[sl][:], self.wf_t[oq])

        for oq in range(4):
            oqs = slice(oq * OQ, (oq + 1) * OQ)
            sl = oq % 2

            def gw(g, oq=oq):
                if oq == 0:
                    load_wf(g, 0)
                if oq + 1 < 4:
                    load_wf(g, oq + 1)
            self.G(gw)

            for rt in range(cfg.RT):
                par = rt % 2

                def yx(s, oq=oq, rt=rt, par=par):
                    self.d_xbt[par].begin(s)
                    if xbt_lastuse[par]:
                        s.wait_ge(self.te.h, xbt_lastuse[par])
                    src = self.x2b_d[rt].rearrange("dc r p -> (dc r) p")
                    st0[(oq, rt, "x")] = self.d_xbt[par].dma(s, x2bT[par][:], src,
                                                             transpose=True)
                self.Y(yx)

                def tm(t, oq=oq, rt=rt, par=par, sl=sl):
                    t.wait_ge(self.d_wf.h, wf_ready[oq])
                    t.wait_ge(self.d_xbt[par].h, st0[(oq, rt, "x")])
                    key = (oq, rt - 2) if rt >= 2 else (oq - 1, cfg.RT - 2 + rt) if oq > 0 else None
                    if key is not None and key in y_done:
                        t.wait_ge(self.ve.h, y_done[key])
                    last = None
                    for dc in range(cfg.HC):
                        for (c0, c1) in _chunks(OQ):
                            last = t.matmul(ps_y[par][:, c0:c1],
                                            x2bT[par][:, dc * 128:(dc + 1) * 128],
                                            wf_sb[sl][:, dc * OQ + c0: dc * OQ + c1],
                                            start=(dc == 0), stop=(dc == cfg.HC - 1))
                    mm_done[(oq, rt)] = self.te.inc(last, 1)
                    xbt_lastuse[par] = mm_done[(oq, rt)]
                    wf_lastuse[sl] = mm_done[(oq, rt)]
                self.T(tm)

                def vy(v, oq=oq, rt=rt, par=par, oqs=oqs):
                    v.wait_ge(self.te.h, mm_done[(oq, rt)])
                    v.wait_ge(self.d_sb.h, st0["bi"])
                    self.d_ys[par].wait(v)
                    t = self.ve.inc(v.tensor_tensor(out=y_sb[par][:], in0=ps_y[par][:, 0:OQ],
                                                    in1=scale_sb[:, oqs], op=ALU.mult), 1)
                    v.wait_ge(self.ve.h, t)
                    y_done[(oq, rt)] = self.ve.inc(
                        v.tensor_tensor(out=y_sb[par][:], in0=y_sb[par][:],
                                        in1=bias_sb[:, oqs], op=ALU.add), 1)
                self.V(vy)

                def yy(s, oq=oq, rt=rt, par=par, oqs=oqs):
                    self.d_ys[par].begin(s)
                    s.wait_ge(self.ve.h, y_done[(oq, rt)])
                    self.d_ys[par].dma(s, self.y_d[rt, :, oqs], y_sb[par][:])
                self.block.scalar(yy)

    def _emit_p3c(self, ln, yload, x2load, v_sb, o_sb):
        cfg = self.cfg
        done = {}
        outs = {}
        for rp in range(0, cfg.RT, 2):
            pair = [rp] + ([rp + 1] if rp + 1 < cfg.RT else [])
            st = {}

            def y1(s, pair=pair):
                for rt in pair:
                    L = rt % 2
                    self.d_yx2.begin(s)
                    if rt - 2 in done:
                        s.wait_ge(self.ve.h, done[rt - 2])
                    self.d_yx2.dma(s, yload[L][:], self.y_d[rt])
                    st[(rt, "x2")] = self.d_yx2.dma(s, x2load[L][:], self.x2_d[rt])
            self.Y(y1)

            def v1(v, pair=pair):
                for rt in pair:
                    L = rt % 2
                    v.wait_ge(self.d_yx2.h, st[(rt, "x2")])
                    if rt - 2 in done:
                        v.wait_ge(self.ve.h, done[rt - 2])
                    if rt - 2 in outs:
                        self.dout.wait(v, outs[rt - 2])
                    t = self.ve.inc(v.tensor_tensor(out=v_sb[L][:], in0=yload[L][:],
                                                    in1=x2load[L][:], op=ALU.add), 1)
                    v.wait_ge(self.ve.h, t)
                    st[(rt, "var")] = self._ln_stats(v, ln[L], v_sb[L])
            self.V(v1)

            def s1(s, pair=pair):
                for rt in pair:
                    L = rt % 2
                    st[(rt, "std")] = self._ln_sqrt(s, ln[L], st[(rt, "var")])
            self.S(s1)

            def v2(v, pair=pair):
                for rt in pair:
                    L = rt % 2
                    done[rt] = self._ln_apply(v, ln[L], v_sb[L], o_sb[L], st[(rt, "std")])
            self.V(v2)

            def y2(s, pair=pair):
                for rt in pair:
                    L = rt % 2
                    self.dout.begin(s)
                    s.wait_ge(self.ve.h, done[rt])
                    outs[rt] = self.dout.dma(s, self.out_d[rt * 128:(rt + 1) * 128, :],
                                             o_sb[L][:])
            self.block.scalar(y2)


def _prep_core_inputs(x_c, host):
    rows, H = x_c.shape
    HC = H // 128
    x_t = np.ascontiguousarray(
        x_c.reshape(2, rows // 2, HC, 128).transpose(0, 3, 2, 1).reshape(2, 128, HC * rows // 2))
    d = {"x_t": x_t, "x_row": np.ascontiguousarray(x_c)}
    d.update(host)
    return d


def _prep_shared(wg, wu, wd, wf, scale, bias, cfg: Cfg):
    I, H, O = cfg.I, cfg.H, cfg.O
    IT, HC, NB, NBW, OH = cfg.IT, cfg.HC, cfg.NB, cfg.NBW, cfg.OH
    wg_t = np.ascontiguousarray(
        wg.reshape(IT, 128, HC, 128).transpose(0, 3, 2, 1).reshape(IT, 128, HC * 128))
    wu_t = np.ascontiguousarray(
        wu.reshape(IT, 128, HC, 128).transpose(0, 3, 2, 1).reshape(IT, 128, HC * 128))
    wd_t = np.ascontiguousarray(
        wd.reshape(NB, NBW, IT, 128).transpose(0, 3, 2, 1))
    OQ = O // 4
    wf_t = np.ascontiguousarray(
        wf.reshape(4, OQ, HC, 128).transpose(0, 3, 2, 1).reshape(4, 128, HC * OQ))
    scale_b = np.ascontiguousarray(np.broadcast_to(scale.reshape(1, O), (128, O)).astype(np.float32))
    bias_b = np.ascontiguousarray(np.broadcast_to(bias.reshape(1, O), (128, O)).astype(np.float32))
    return {"wg_t": wg_t, "wu_t": wu_t, "wd_t": wd_t, "wf_t": wf_t,
            "scale_b": scale_b, "bias_b": bias_b}


def kernel(L0_attn_x, L0_attn_w_gate, L0_attn_w_up, L0_attn_w_down,
           L0_ffn_weight_int8, L0_ffn_weight_scale, L0_ffn_bias):
    cfg = Cfg()
    x = np.ascontiguousarray(np.asarray(L0_attn_x, dtype=np.float32)).reshape(-1, cfg.H)
    wg = np.ascontiguousarray(np.asarray(L0_attn_w_gate, dtype=np.float32))
    wu = np.ascontiguousarray(np.asarray(L0_attn_w_up, dtype=np.float32))
    wd = np.ascontiguousarray(np.asarray(L0_attn_w_down, dtype=np.float32))
    wf = np.ascontiguousarray(np.asarray(L0_ffn_weight_int8, dtype=np.int8))
    scale = np.asarray(L0_ffn_weight_scale, dtype=np.float32)
    bias = np.asarray(L0_ffn_bias, dtype=np.float32)

    shared = _prep_shared(wg, wu, wd, wf, scale, bias, cfg)
    in_maps = []
    for c in range(N_CORES):
        x_c = x[c * cfg.rows:(c + 1) * cfg.rows]
        in_maps.append(_prep_core_inputs(x_c, shared))

    nc = Builder(cfg).build()
    res = run_bass_kernel_spmd(nc, in_maps, core_ids=list(range(N_CORES)))
    out = np.concatenate([res.results[c]["out"] for c in range(N_CORES)], axis=0)
    B, S = np.asarray(L0_attn_x).shape[:2]
    return out.reshape(B, S, cfg.H).astype(np.float32)


# version 10
# speedup vs baseline: 1.0061x; 1.0061x over previous
import dataclasses
from contextlib import ExitStack

import numpy as np

import concourse.bass as bass
import concourse.mybir as mybir
from concourse.bass_utils import run_bass_kernel_spmd

F32 = mybir.dt.float32
BF16 = mybir.dt.bfloat16
F16 = mybir.dt.float16
I8 = mybir.dt.int8
AF = mybir.ActivationFunctionType
ALU = mybir.AluOpType
AX = mybir.AxisListType

N_CORES = 8


@dataclasses.dataclass
class Cfg:
    rows: int = 2048
    H: int = 4096
    I: int = 11008
    O: int = 4096
    n_iter: int = 13
    eps: float = 1e-5

    @property
    def HC(self):  return self.H // 128
    @property
    def IT(self):  return self.I // 128
    @property
    def RT(self):  return self.rows // 128
    @property
    def RTH(self): return self.RT // 2
    @property
    def RH(self):  return self.rows // 2
    @property
    def NBW(self): return min(512, self.H)
    @property
    def NB(self):  return self.H // self.NBW
    @property
    def OH(self):  return self.O // 2
    @property
    def k(self):   return self.I // 2
    @property
    def IV(self):  return min((self.IT * 60 // 128) * 128, self.I - 128) if self.I > 1024 else (self.IT // 2) * 128
    @property
    def IS(self):  return self.I - self.IV


def _chunks(n, c=512):
    return [(i, min(i + c, n)) for i in range(0, n, c)]


class Sem:

    def __init__(self, st, nc, name):
        self.h = st.enter_context(nc.semaphore(name))
        self.n = 0

    def inc(self, inst, k=1):
        inst.then_inc(self.h, k)
        self.n += k
        return self.n


class Stream:

    def __init__(self, st, nc, name):
        self.h = st.enter_context(nc.semaphore(name))
        self.n = 0

    def begin(self, e):
        if self.n:
            e.wait_ge(self.h, self.n)

    def dma(self, e, out, in_, transpose=False):
        if transpose:
            inst = e.dma_start(out=out, in_=in_, transpose=True)
        else:
            inst = e.dma_start(out=out, in_=in_)
        inst.then_inc(self.h, 16)
        self.n += 16
        return self.n

    def wait(self, e, tgt=None):
        t = self.n if tgt is None else tgt
        if t:
            e.wait_ge(self.h, t)


class Builder:
    def __init__(self, cfg: Cfg):
        self.cfg = cfg
        self.nc = bass.Bass("TRN2", target_bir_lowering=False)

    def T(self, f):  self.block.tensor(f)

    def V(self, f):
        def g(v):
            if self.ve.n:
                v.wait_ge(self.ve.h, self.ve.n)
            f(v)
        self.block.vector(g)

    def S(self, f):
        def g(s):
            if self.sc.n:
                s.wait_ge(self.sc.h, self.sc.n)
            f(s)
        self.block.scalar(g)

    def G(self, f):  self.block.gpsimd(f)
    def Y(self, f):  self.block.sync(f)

    def build(self):
        nc, cfg = self.nc, self.cfg
        self.x_t = nc.dram_tensor("x_t", [2, 128, cfg.HC * cfg.RH], F32, kind="ExternalInput").ap()
        self.x_row = nc.dram_tensor("x_row", [cfg.rows, cfg.H], F32, kind="ExternalInput").ap()
        self.wg_t = nc.dram_tensor("wg_t", [cfg.IT, 128, cfg.HC * 128], F32, kind="ExternalInput").ap()
        self.wu_t = nc.dram_tensor("wu_t", [cfg.IT, 128, cfg.HC * 128], F32, kind="ExternalInput").ap()
        self.wd_t = nc.dram_tensor("wd_t", [cfg.NB, 128, cfg.IT, cfg.NBW], F32, kind="ExternalInput").ap()
        self.wf_t = nc.dram_tensor("wf_t", [4, 128, cfg.HC * (cfg.O // 4)], I8, kind="ExternalInput").ap()
        self.scale_b = nc.dram_tensor("scale_b", [128, cfg.O], F32, kind="ExternalInput").ap()
        self.bias_b = nc.dram_tensor("bias_b", [128, cfg.O], F32, kind="ExternalInput").ap()
        self.out_d = nc.dram_tensor("out", [cfg.rows, cfg.H], F32, kind="ExternalOutput").ap()
        self.aT_d = nc.dram_tensor("aT_s", [cfg.IT, 128, cfg.rows], F16, kind="Internal").ap()
        self.pT_d = nc.dram_tensor("pT_s", [cfg.IT, 128, cfg.rows], BF16, kind="Internal").ap()
        self.pm_d = nc.dram_tensor("pm_s", [cfg.RT, 128, cfg.IT, 128], BF16, kind="Internal").ap()
        self.tflat_d = nc.dram_tensor("tflat_s", [1, cfg.rows], F32, kind="Internal").ap()
        self.C_d = nc.dram_tensor("C_s", [cfg.RT, 128, cfg.H], BF16, kind="Internal").ap()
        self.x2_d = nc.dram_tensor("x2_s", [cfg.RT, 128, cfg.H], BF16, kind="Internal").ap()
        self.x2b_d = nc.dram_tensor("x2b_s", [cfg.RT, cfg.HC, 128, 128], BF16, kind="Internal").ap()
        self.y_d = nc.dram_tensor("y_s", [cfg.RT, 128, cfg.O], BF16, kind="Internal").ap()

        with ExitStack() as st:
            self.te = Sem(st, nc, "te")
            self.ve = Sem(st, nc, "ve")
            self.sc = Sem(st, nc, "sc")
            self.dout = Stream(st, nc, "dout")
            self.d_x = Stream(st, nc, "d_x")
            self.d_w = [Stream(st, nc, f"d_w{i}") for i in range(2)]
            self.d_wd = Stream(st, nc, "d_wd")
            self.d_wf = Stream(st, nc, "d_wf")
            self.d_as = [Stream(st, nc, f"d_as{i}") for i in range(2)]
            self.d_ps = [Stream(st, nc, f"d_ps{i}") for i in range(2)]
            self.d_ar = [Stream(st, nc, f"d_ar{i}") for i in range(2)]
            self.d_hi = Stream(st, nc, "d_hi")
            self.d_tb = Stream(st, nc, "d_tb")
            self.d_mk = [Stream(st, nc, f"d_mk{i}") for i in range(3)]
            self.d_pmc = [Stream(st, nc, f"d_pmc{i}") for i in range(3)]
            self.d_pm = [Stream(st, nc, f"d_pm{i}") for i in range(2)]
            self.d_cs = [Stream(st, nc, f"d_cs{i}") for i in range(2)]
            self.d_cx = Stream(st, nc, "d_cx")
            self.d_x2 = Stream(st, nc, "d_x2")
            self.d_sb = Stream(st, nc, "d_sb")
            self.d_xbt = [Stream(st, nc, f"d_xbt{i}") for i in range(2)]
            self.d_ys = [Stream(st, nc, f"d_ys{i}") for i in range(2)]
            self.d_yx2 = Stream(st, nc, "d_yx2")
            self.all_sems = ([self.te, self.ve, self.sc, self.dout, self.d_x,
                              self.d_wd, self.d_wf, self.d_hi, self.d_tb,
                              self.d_cx, self.d_x2, self.d_sb, self.d_yx2]
                             + self.d_w + self.d_as + self.d_ps + self.d_ar
                             + self.d_mk + self.d_pmc + self.d_pm + self.d_cs
                             + self.d_xbt + self.d_ys)

            with nc.Block() as block:
                self.block = block
                self._emit_all(st)
        return nc

    def _emit_all(self, st):
        nc, cfg = self.nc, self.cfg
        with ExitStack() as p1s:
            sb = lambda name, shape, dt: p1s.enter_context(nc.sbuf_tensor(name, shape, dt))
            self.xT = sb("xT", [128, cfg.HC * cfg.RH], BF16)
            self.wgb = [sb(f"wgb{i}", [128, cfg.HC * 128], BF16) for i in range(2)]
            self.wub = [sb(f"wub{i}", [128, cfg.HC * 128], BF16) for i in range(2)]
            self.s_sb = [sb(f"s_sb{i}", [128, cfg.RH], F32) for i in range(2)]
            self.a_sb = [sb(f"a_sb{i}", [128, cfg.RH], F16) for i in range(2)]
            self.p_sb = [sb(f"p_sb{i}", [128, cfg.RH], BF16) for i in range(2)]
            self.a_row = [sb(f"a_row{i}", [128, cfg.I], F16) for i in range(2)]
            self.dump = sb("dump", [128, cfg.I], F16)
            self.lo = [[sb(f"lo{l}{i}", [128, 1], F32) for i in range(2)] for l in range(2)]
            self.hi = [[sb(f"hi{l}{i}", [128, 1], F32) for i in range(2)] for l in range(2)]
            self.mid = [sb(f"mid{l}", [128, 1], F32) for l in range(2)]
            self.mid2 = [sb(f"mid2{l}", [128, 1], F32) for l in range(2)]
            self.cnt_v = [sb(f"cnt_v{l}", [128, 1], F32) for l in range(2)]
            self.S_s = [sb(f"S_s{l}", [128, 1], F32) for l in range(2)]
            self.cond = [sb(f"cond{l}", [128, 1], F32) for l in range(2)]
            self.ge_m = [sb(f"ge_m{l}", [128, 1], mybir.dt.uint32) for l in range(2)]
            self.t_bc = sb("t_bc", [128, cfg.RH], F32)
            self.t_bc16 = sb("t_bc16", [128, cfg.RH], F16)
            self.aT_c = [sb(f"aT_c{i}", [128, cfg.RH], F16) for i in range(3)]
            self.pT_c = [sb(f"pT_c{i}", [128, cfg.RH], BF16) for i in range(3)]
            self.m_c = [sb(f"m_c{i}", [128, cfg.RH], BF16) for i in range(3)]
            self.pm_c = [sb(f"pm_c{i}", [128, cfg.RH], BF16) for i in range(3)]
            with ExitStack() as ps1:
                pt = lambda name: ps1.enter_context(
                    nc.psum_tensor(name, [128, max(cfg.RH, 512)], F32))
                self.ps_g = [pt("ps_g0"), pt("ps_g1")]
                self.ps_u = [pt("ps_u0"), pt("ps_u1")]

                self.wg_ready = {}; self.wu_ready = {}; self.x_ready = {}
                self.g_done = {}; self.u_done = {}
                self.silu_done = {}; self.abs_done = {}; self.pmul_done = {}
                self.w_slot_free = [0, 0]
                self.bi_v = 0
                self.bi_lane_v = [0, 0]
                self.unit_mode = "split"
                self.pulled_cost = 0.0
                self.mid2_t = 0
                self.ss_t = 0
                self.cntv_t = 0
                self.a_ready = [0, 0]
                self.a_lastuse_v = [0, 0]
                self.a_lastuse_s = [0, 0]
                self.tb16_t = 0
                self.mask_lastuse = [0, 0, 0]
                self.as_fence = {}
                self.ps_fence = {}
                self.units = None

                with nc.named_scope("p1h0"):
                    self._emit_p1(0)
                self.pulled_cost = 0.0
                self.units = iter(list(self._p15_units(0)))
                with nc.named_scope("p1h1"):
                    self._emit_p1(1)
                    self.unit_mode = "split"
                    self._drain_units()
                with nc.named_scope("p15h1"):
                    self.units = iter(list(self._p15_units(1)))
                    self._drain_units()
        self._emit_join()

        with ExitStack() as p2s:
            sb = lambda name, shape, dt: p2s.enter_context(nc.sbuf_tensor(name, shape, dt))
            ITH = (cfg.IT + 1) // 2
            self.wd_sl = [sb(f"wd_sl{i}", [128, ITH * cfg.NBW], BF16) for i in range(3)]
            self.pm_sb = [sb(f"pm_sb{i}", [128, cfg.IT * 128], BF16) for i in range(2)]
            self.c_sb = [sb(f"c_sb{i}", [128, cfg.NBW], BF16) for i in range(2)]
            with ExitStack() as ps2:
                self.ps_c = [ps2.enter_context(nc.psum_tensor(f"ps_c{i}", [128, 512], F32))
                             for i in range(2)]
                with nc.named_scope("p2"):
                    self._emit_p2()
        self._emit_join()

        with ExitStack() as p3as:
            sb = lambda name, shape, dt: p3as.enter_context(nc.sbuf_tensor(name, shape, dt))
            ln = [self._alloc_ln(sb, pfx=f"a{i}") for i in range(2)]
            cload = [sb(f"cload{i}", [128, cfg.H], BF16) for i in range(2)]
            xload = [sb(f"xload{i}", [128, cfg.H], F32) for i in range(2)]
            v_sb = [sb(f"v_sb{i}", [128, cfg.H], F32) for i in range(2)]
            x2bbuf = [sb(f"x2bbuf{i}", [128, cfg.H], BF16) for i in range(2)]
            with nc.named_scope("p3a"):
                self._emit_p3a(ln, cload, xload, v_sb, None, x2bbuf)
        self._emit_join()

        with ExitStack() as p3bs:
            sb = lambda name, shape, dt: p3bs.enter_context(nc.sbuf_tensor(name, shape, dt))
            OQ = cfg.O // 4
            wf_sb = [sb(f"wf_sb{i}", [128, cfg.HC * OQ], BF16) for i in range(2)]
            x2bT = [sb(f"x2bT{i}", [128, cfg.HC * 128], BF16) for i in range(2)]
            scale_sb = sb("scale_sb", [128, cfg.O], F32)
            bias_sb = sb("bias_sb", [128, cfg.O], F32)
            y_sb = [sb(f"y_sb{i}", [128, OQ], F32) for i in range(2)]
            yb_sb = [sb(f"yb_sb{i}", [128, OQ], BF16) for i in range(2)]
            with ExitStack() as ps3:
                ps_y = [ps3.enter_context(nc.psum_tensor(f"ps_y{i}", [128, max(OQ, 512)], F32))
                        for i in range(2)]
                with nc.named_scope("p3b"):
                    self._emit_p3b(wf_sb, x2bT, scale_sb, bias_sb, y_sb, yb_sb, ps_y)
        self._emit_join()

        with ExitStack() as p3cs:
            sb = lambda name, shape, dt: p3cs.enter_context(nc.sbuf_tensor(name, shape, dt))
            ln = [self._alloc_ln(sb, pfx=f"c{i}") for i in range(2)]
            yload = [sb(f"yload{i}", [128, cfg.O], BF16) for i in range(2)]
            x2load = [sb(f"x2load{i}", [128, cfg.H], BF16) for i in range(2)]
            v_sb = [sb(f"v_sbc{i}", [128, cfg.H], F32) for i in range(2)]
            o_sb = [sb(f"o_sb{i}", [128, cfg.H], F32) for i in range(2)]
            with nc.named_scope("p3c"):
                self._emit_p3c(ln, yload, x2load, v_sb, o_sb)

        total = self.dout.n
        self.Y(lambda s: s.wait_ge(self.dout.h, total))

    def _emit_join(self):
        tgts = [(s.h, s.n) for s in self.all_sems if s.n > 0]
        for emit in (self.T, self.V, self.S, self.G, self.Y):
            def f(e, tgts=tgts):
                for h, n in tgts:
                    e.wait_ge(h, n)
            emit(f)

    def _emit_p1(self, half):
        cfg = self.cfg
        hs = slice(half * cfg.RH, (half + 1) * cfg.RH)

        def gx(g, half=half):
            self.d_x.begin(g)
            if half == 1:
                g.wait_ge(self.te.h, self.u_done[(0, cfg.IT - 1)])
            src = self.x_t[half]
            n = cfg.HC * cfg.RH
            ngr = 4 if n % 4 == 0 else 1
            step = n // ngr
            for j in range(ngr):
                self.d_x.dma(g, self.xT[:, j*step:(j+1)*step], src[:, j*step:(j+1)*step])
            self.x_ready[half] = self.d_x.n
        self.G(gx)

        for it in range(cfg.IT):
            slot = (half * cfg.IT + it) % 2
            pset = it % 2

            def gw(g, it=it, slot=slot, half=half):
                self.d_w[slot].begin(g)
                if self.w_slot_free[slot] > 0:
                    g.wait_ge(self.te.h, self.w_slot_free[slot])
                self.d_w[slot].dma(g, self.wgb[slot][:], self.wg_t[it])
                self.wu_ready[(half, it)] = self.d_w[slot].dma(g, self.wub[slot][:], self.wu_t[it])
            self.G(gw)

            def tm(t, it=it, slot=slot, pset=pset, half=half):
                t.wait_ge(self.d_w[slot].h, self.wu_ready[(half, it)])
                t.wait_ge(self.d_x.h, self.x_ready[half])
                prev = (half, it - 2) if it >= 2 else (half - 1, cfg.IT - 2 + it) if half == 1 else None
                if prev is not None and prev in self.silu_done:
                    t.wait_ge(self.sc.h, self.silu_done[prev])
                    t.wait_ge(self.ve.h, self.pmul_done[prev])
                ck = _chunks(cfg.RH)
                for mat, wsb, ps in ((0, self.wgb[slot], self.ps_g[pset]),
                                     (1, self.wub[slot], self.ps_u[pset])):
                    last = None
                    for hc in range(cfg.HC):
                        for (c0, c1) in ck:
                            last = t.matmul(
                                ps[:, c0:c1],
                                wsb[:, hc * 128:(hc + 1) * 128],
                                self.xT[:, hc * cfg.RH + c0: hc * cfg.RH + c1],
                                start=(hc == 0), stop=(hc == cfg.HC - 1))
                    tgt = self.te.inc(last, 1)
                    if mat == 0:
                        self.g_done[(half, it)] = tgt
                    else:
                        self.u_done[(half, it)] = tgt
                self.w_slot_free[slot] = self.u_done[(half, it)]
            self.T(tm)

            def sca(s, it=it, pset=pset, half=half):
                s.wait_ge(self.te.h, self.g_done[(half, it)])
                prev = (half, it - 2) if it >= 2 else (half - 1, cfg.IT - 2 + it) if half == 1 else None
                if prev is not None and prev in self.abs_done:
                    s.wait_ge(self.ve.h, self.abs_done[prev])
                self.silu_done[(half, it)] = self.sc.inc(
                    s.activation(out=self.s_sb[pset][:],
                                 in_=self.ps_g[pset][:, 0:cfg.RH],
                                 func=AF.Silu), 1)
            self.S(sca)

            def vp(v, it=it, pset=pset, half=half):
                v.wait_ge(self.te.h, self.u_done[(half, it)])
                v.wait_ge(self.sc.h, self.silu_done[(half, it)])
                self.d_ps[pset].wait(v)
                self.pmul_done[(half, it)] = self.ve.inc(
                    v.tensor_tensor(out=self.p_sb[pset][:], in0=self.s_sb[pset][:],
                                    in1=self.ps_u[pset][:, 0:cfg.RH], op=ALU.mult), 1)
                self.d_as[pset].wait(v)
                self.abs_done[(half, it)] = self.ve.inc(
                    v.tensor_tensor(out=self.a_sb[pset][:], in0=self.s_sb[pset][:],
                                    in1=self.s_sb[pset][:], op=ALU.mult), 1)
            self.V(vp)

            def ys(s, it=it, pset=pset, half=half):
                self.d_as[pset].begin(s)
                s.wait_ge(self.ve.h, self.abs_done[(half, it)])
                self.d_as[pset].dma(s, self.aT_d[it, :, hs], self.a_sb[pset][:])
                self.d_ps[pset].begin(s)
                s.wait_ge(self.ve.h, self.pmul_done[(half, it)])
                self.d_ps[pset].dma(s, self.pT_d[it, :, hs], self.p_sb[pset][:])
            self.Y(ys)

            self._pull_units_budget((it + 1) * 16.0)
        self.as_fence[half] = (self.d_as[0].n, self.d_as[1].n)
        self.ps_fence[half] = (self.d_ps[0].n, self.d_ps[1].n)

    def _pull_units_budget(self, budget):
        if self.units is None:
            return
        while self.pulled_cost < budget:
            u = next(self.units, None)
            if u is None:
                self.units = None
                return
            c, f = u
            f()
            self.pulled_cost += c

    def _drain_units(self):
        if self.units is None:
            return
        for u in self.units:
            u[1]()
        self.units = None

    def _p15_units(self, half):
        cfg = self.cfg
        for rp in range(0, cfg.RTH, 2):
            pair = [half * cfg.RTH + rp]
            if rp + 1 < cfg.RTH:
                pair.append(half * cfg.RTH + rp + 1)
            for rt in pair:
                yield (12.0, lambda rt=rt: self._u_binit(rt))
            for j in range(cfg.n_iter):
                yield (17.0, lambda pair=tuple(pair), j=j: self._u_biter_pair(pair, j))
            for rt in pair:
                yield (2.0, lambda rt=rt: self._u_bfin(rt))
        yield (3.0, lambda half=half: self._u_tbcast(half))
        for it in range(cfg.IT):
            yield (3.0, lambda it=it: self._u_mask(half, it))

    def _u_binit(self, rt):
        cfg = self.cfg
        ab = rt % 2
        half = rt // cfg.RTH

        def y(s, rt=rt, ab=ab, half=half):
            self.d_ar[ab].begin(s)
            for i in (0, 1):
                s.wait_ge(self.d_as[i].h, self.as_fence[half][i])
            if self.a_lastuse_v[ab]:
                s.wait_ge(self.ve.h, self.a_lastuse_v[ab])
            if self.a_lastuse_s[ab]:
                s.wait_ge(self.sc.h, self.a_lastuse_s[ab])
            src = self.aT_d.rearrange("it p r -> (it p) r")[:, rt * 128:(rt + 1) * 128]
            self.a_ready[ab] = self.d_ar[ab].dma(s, self.a_row[ab][:], src, transpose=True)
        self.Y(y)

        def v(v, rt=rt, ab=ab):
            v.wait_ge(self.d_ar[ab].h, self.a_ready[ab])
            self.d_hi.wait(v)
            if self.bi_lane_v[ab]:
                v.wait_ge(self.ve.h, self.bi_lane_v[ab])
            self.ve.inc(v.reduce_max(out=self.hi[ab][0][:], in_=self.a_row[ab][:], axis=AX.X), 1)
            self.ve.inc(v.memset(self.lo[ab][0][:], 0.0), 1)
            self.bi_lane_v[ab] = self.ve.n
        self.V(v)

    def _u_biter_pair(self, pair, j):
        cfg = self.cfg
        p, q = j % 2, 1 - (j % 2)
        vonly = self.unit_mode == "vonly"
        nv = cfg.I if vonly else cfg.IV
        w = (2.0 ** -9, 2.0 ** -6, 2.0 ** -3)[j] if j < 3 else 0.5
        mid2_t = {}
        cntv_t = {}
        ss_t = {}

        def v1(v):
            for rt in pair:
                L = rt % 2
                v.wait_ge(self.ve.h, self.bi_lane_v[L])
                t = self.ve.inc(v.tensor_sub(self.mid[L][:], self.hi[L][p][:], self.lo[L][p][:]), 1)
                v.wait_ge(self.ve.h, t)
                mid2_t[L] = self.ve.inc(
                    v.scalar_tensor_tensor(out=self.mid2[L][:], in0=self.mid[L][:],
                                           scalar=w, in1=self.lo[L][p][:],
                                           op0=ALU.mult, op1=ALU.add), 1)
            for rt in pair:
                L = rt % 2
                v.wait_ge(self.ve.h, mid2_t[L])
                if cntv_t:
                    v.wait_ge(self.ve.h, max(cntv_t.values()))
                cntv_t[L] = self.ve.inc(
                    v.tensor_scalar(out=self.dump[:, 0:nv], in0=self.a_row[L][:, 0:nv],
                                    scalar1=self.mid2[L][:], scalar2=None,
                                    op0=ALU.is_le, op1=ALU.add,
                                    accum_out=self.cnt_v[L][:]), 1)
                self.a_lastuse_v[L] = cntv_t[L]
        self.V(v1)

        if not vonly:
            def s1(s):
                for rt in pair:
                    L = rt % 2
                    s.wait_ge(self.ve.h, mid2_t[L])
                    if ss_t:
                        s.wait_ge(self.sc.h, max(ss_t.values()))
                    ss_t[L] = self.sc.inc(
                        s.activation(out=self.dump[:, cfg.IV:cfg.I],
                                     in_=self.a_row[L][:, cfg.IV:cfg.I],
                                     func=AF.Sign, scale=-1.0, bias=self.mid2[L][:],
                                     accum_out=self.S_s[L][:]), 1)
                    self.a_lastuse_s[L] = ss_t[L]
            self.S(s1)

        def v2(v):
            for rt in pair:
                L = rt % 2
                if vonly:
                    v.wait_ge(self.ve.h, cntv_t[L])
                    t = self.ve.inc(v.tensor_single_scalar(
                        out=self.ge_m[L][:], in_=self.cnt_v[L][:],
                        scalar=float(cfg.k), op=ALU.is_ge), 1)
                else:
                    v.wait_ge(self.sc.h, ss_t[L])
                    v.wait_ge(self.ve.h, cntv_t[L])
                    t = self.ve.inc(v.scalar_tensor_tensor(
                        out=self.cond[L][:], in0=self.S_s[L][:], scalar=0.5,
                        in1=self.cnt_v[L][:], op0=ALU.mult, op1=ALU.add), 1)
                    v.wait_ge(self.ve.h, t)
                    t = self.ve.inc(v.tensor_single_scalar(
                        out=self.ge_m[L][:], in_=self.cond[L][:],
                        scalar=float(cfg.k - cfg.IS / 2.0), op=ALU.is_ge), 1)
                v.wait_ge(self.ve.h, t)
                t = self.ve.inc(v.tensor_copy(out=self.hi[L][q][:], in_=self.hi[L][p][:]), 1)
                v.wait_ge(self.ve.h, t)
                t = self.ve.inc(v.copy_predicated(out=self.hi[L][q][:], mask=self.ge_m[L][:],
                                                  data=self.mid2[L][:]), 1)
                v.wait_ge(self.ve.h, t)
                t = self.ve.inc(v.tensor_copy(out=self.lo[L][q][:], in_=self.mid2[L][:]), 1)
                v.wait_ge(self.ve.h, t)
                t = self.ve.inc(v.copy_predicated(out=self.lo[L][q][:], mask=self.ge_m[L][:],
                                                  data=self.lo[L][p][:]), 1)
                self.bi_lane_v[L] = self.ve.n
        self.V(v2)

    def _u_bfin(self, rt):
        p = self.cfg.n_iter % 2
        L = rt % 2

        def y(s, rt=rt, p=p, L=L):
            self.d_hi.begin(s)
            s.wait_ge(self.ve.h, self.bi_lane_v[L])
            dst = self.tflat_d[0:1, rt * 128:(rt + 1) * 128].rearrange("one p -> p one")
            self.d_hi.dma(s, dst, self.hi[L][p][:])
        self.Y(y)

    def _u_tbcast(self, half):
        cfg = self.cfg

        def y(s, half=half):
            self.d_tb.begin(s)
            self.d_hi.wait(s)
            base = self.tflat_d[0:1, half * cfg.RH:(half + 1) * cfg.RH]
            src = dataclasses.replace(base, ap=[[0, 128]] + list(base.ap[1:]))
            if self.tb16_t:
                s.wait_ge(self.ve.h, self.tb16_t)
            self.d_tb.dma(s, self.t_bc[:], src)
        self.Y(y)

        def v(v):
            self.d_tb.wait(v)
            self.tb16_t = self.ve.inc(v.tensor_copy(out=self.t_bc16[:], in_=self.t_bc[:]), 1)
        self.V(v)

    def _u_mask(self, half, it):
        cfg = self.cfg
        mb = it % 3
        hs = slice(half * cfg.RH, (half + 1) * cfg.RH)

        def y(s, it=it, mb=mb, half=half):
            self.d_mk[mb].begin(s)
            for i in (0, 1):
                s.wait_ge(self.d_ps[i].h, self.ps_fence[half][i])
            if self.mask_lastuse[mb]:
                s.wait_ge(self.ve.h, self.mask_lastuse[mb])
            self.d_mk[mb].dma(s, self.aT_c[mb][:], self.aT_d[it, :, hs])
            self.mk_t = self.d_mk[mb].dma(s, self.pT_c[mb][:], self.pT_d[it, :, hs])
        self.Y(y)

        def v(v, it=it, mb=mb):
            v.wait_ge(self.d_mk[mb].h, self.mk_t)
            v.wait_ge(self.ve.h, self.tb16_t)
            self.d_pmc[mb].wait(v)
            t = self.ve.inc(v.tensor_tensor(out=self.m_c[mb][:], in0=self.aT_c[mb][:],
                                            in1=self.t_bc16[:], op=ALU.is_gt), 1)
            v.wait_ge(self.ve.h, t)
            self.pm_t = self.ve.inc(v.tensor_tensor(out=self.pm_c[mb][:], in0=self.pT_c[mb][:],
                                                    in1=self.m_c[mb][:], op=ALU.mult), 1)
            self.mask_lastuse[mb] = self.pm_t
        self.V(v)

        def y2(s, it=it, mb=mb, half=half):
            self.d_pmc[mb].begin(s)
            s.wait_ge(self.ve.h, self.pm_t)
            dst = self.pm_d[half * cfg.RTH:(half + 1) * cfg.RTH, :, it, :] \
                .rearrange("rt p c -> p rt c")
            src = self.pm_c[mb][:].rearrange("p (rt c) -> p rt c", rt=cfg.RTH)
            self.d_pmc[mb].dma(s, dst, src)
        self.block.scalar(y2)

    def _emit_p2(self):
        cfg = self.cfg
        ITH = (cfg.IT + 1) // 2
        nhp = 2 if cfg.IT > ITH else 1
        te0, ve0 = self.te.n, self.ve.n
        wd0 = self.d_wd.n
        pm0 = [s.n for s in self.d_pm]
        cs0 = [s.n for s in self.d_cs]

        NRT = cfg.RT

        def mm_done(nb, r):
            return te0 + nb * NRT + r + 1

        def evac_done(nb, r):
            return ve0 + nb * NRT + r + 1

        def pm_ready(nb, r):
            par = r % 2
            return pm0[par] + 16 * (nb * (NRT // 2) + r // 2 + 1)

        wd_seq = [(0, 0)]
        for nb in range(cfg.NB):
            if nhp > 1:
                wd_seq.append((nb, 1))
            if nb + 1 < cfg.NB:
                wd_seq.append((nb + 1, 0))
        wd_ready = {}
        for j, key in enumerate(wd_seq):
            wd_ready[key] = wd0 + 16 * (j + 1)

        def slot_of(nb, sh):
            return (nb * nhp + sh) % 3

        wd_src = self.wd_t.rearrange("nb p it h -> nb p (it h)")

        def g_all(g):
            done = set()
            def load_hp(nb, sh):
                sl = slot_of(nb, sh)
                self.d_wd.begin(g)
                idx = nb * nhp + sh
                if idx >= 3:
                    pnb = (idx - 3) // nhp
                    g.wait_ge(self.te.h, mm_done(pnb, NRT - 1))
                i0 = sh * ITH
                i1 = min(cfg.IT, i0 + ITH)
                cols = (i1 - i0) * cfg.NBW
                srcap = wd_src[nb][:, i0 * cfg.NBW: i0 * cfg.NBW + cols]
                t = self.d_wd.dma(g, self.wd_sl[sl][:, 0:cols], srcap)
                assert t == wd_ready[(nb, sh)], (t, wd_ready[(nb, sh)])
            for nb in range(cfg.NB):
                if nb == 0:
                    load_hp(0, 0)
                if nhp > 1:
                    load_hp(nb, 1)
                if nb + 1 < cfg.NB:
                    load_hp(nb + 1, 0)
        self.G(g_all)

        def y_all(s):
            for nb in range(cfg.NB):
                for r in range(NRT):
                    par = r % 2
                    self.d_pm[par].begin(s)
                    if nb * NRT + r >= 2:
                        pr = (nb * NRT + r) - 2
                        s.wait_ge(self.te.h, mm_done(pr // NRT, pr % NRT))
                    t = self.d_pm[par].dma(s, self.pm_sb[par][:],
                                           self.pm_d[r].rearrange("p it c -> p (it c)"))
                    assert t == pm_ready(nb, r)
        def y_all2(s):
            for nb in range(cfg.NB):
                for r in range(NRT):
                    par = r % 2
                    self.d_pm[par].begin(s)
                    if nb * NRT + r >= 2:
                        pr = (nb * NRT + r) - 2
                        s.wait_ge(self.te.h, mm_done(pr // NRT, pr % NRT))
                    t = self.d_pm[par].dma(s, self.pm_sb[par][:],
                                           self.pm_d[r].rearrange("p it c -> p (it c)"))
                    assert t == pm_ready(nb, r), (t, pm_ready(nb, r))
        self.Y(y_all2)

        def a_all(s):
            for nb in range(cfg.NB):
                for r in range(NRT):
                    par = r % 2
                    self.d_cs[par].begin(s)
                    s.wait_ge(self.ve.h, evac_done(nb, r))
                    self.d_cs[par].dma(s, self.C_d[r, :, nb * cfg.NBW:(nb + 1) * cfg.NBW],
                                       self.c_sb[par][:])
        self.block.scalar(a_all)

        def t_all(t):
            for nb in range(cfg.NB):
                for r in range(NRT):
                    par = r % 2
                    t.wait_ge(self.d_wd.h, wd_ready[(nb, 0)])
                    t.wait_ge(self.d_pm[par].h, pm_ready(nb, r))
                    idx = nb * NRT + r
                    if idx >= 2:
                        pr = idx - 2
                        t.wait_ge(self.ve.h, evac_done(pr // NRT, pr % NRT))
                    last = None
                    for it in range(cfg.IT):
                        if it == ITH and nhp > 1:
                            t.wait_ge(self.d_wd.h, wd_ready[(nb, 1)])
                        sl = slot_of(nb, it // ITH)
                        io = it % ITH
                        last = t.matmul(self.ps_c[par][:, 0:cfg.NBW],
                                        self.pm_sb[par][:, it * 128:(it + 1) * 128],
                                        self.wd_sl[sl][:, io * cfg.NBW:(io + 1) * cfg.NBW],
                                        start=(it == 0), stop=(it == cfg.IT - 1))
                    tgt = self.te.inc(last, 1)
                    assert tgt == mm_done(nb, r)
        self.T(t_all)

        def v_all(v):
            for nb in range(cfg.NB):
                for r in range(NRT):
                    par = r % 2
                    v.wait_ge(self.te.h, mm_done(nb, r))
                    idx = nb * NRT + r
                    if idx >= 2:
                        pr = idx - 2
                        pp = pr % 2
                        self.d_cs[pp].wait(v, cs0[pp] + 16 * (pr // 2 + 1))
                    tgt = self.ve.inc(
                        v.tensor_copy(out=self.c_sb[par][:], in_=self.ps_c[par][:, 0:cfg.NBW]), 1)
                    assert tgt == evac_done(nb, r)
        self.V(v_all)

    def _alloc_ln(self, sb, pfx=""):
        cfg = self.cfg
        return {
            "srow": sb(pfx + "srow", [128, 1], F32),
            "m": sb(pfx + "m", [128, 1], F32),
            "ssq": sb(pfx + "ssq", [128, 1], F32),
            "msq": sb(pfx + "msq", [128, 1], F32),
            "varb": sb(pfx + "varb", [128, 1], F32),
            "stdb": sb(pfx + "stdb", [128, 1], F32),
            "rstd": sb(pfx + "rstd", [128, 1], F32),
            "nmr": sb(pfx + "nmr", [128, 1], F32),
            "sqdump": sb(pfx + "sqdump", [128, cfg.H], F32),
        }

    def _emit_ln(self, ln, v_in, x2_out, ready_t):
        cfg = self.cfg
        res = {}

        def v1(v):
            v.wait_ge(self.ve.h, ready_t)
            t = self.ve.inc(v.reduce_sum(out=ln["srow"][:], in_=v_in[:], axis=AX.X), 1)
            t = self.ve.inc(v.scalar_tensor_tensor(
                out=ln["sqdump"][:], in0=v_in[:], scalar=1.0, in1=v_in[:],
                op0=ALU.mult, op1=ALU.mult, accum_out=ln["ssq"][:]), 1)
            v.wait_ge(self.ve.h, t)
            t = self.ve.inc(v.tensor_scalar_mul(ln["m"][:], ln["srow"][:], 1.0 / cfg.H), 1)
            v.wait_ge(self.ve.h, t)
            t = self.ve.inc(v.tensor_mul(ln["msq"][:], ln["m"][:], ln["m"][:]), 1)
            v.wait_ge(self.ve.h, t)
            t = self.ve.inc(v.tensor_scalar(out=ln["varb"][:], in0=ln["ssq"][:],
                                            scalar1=1.0 / cfg.H, scalar2=cfg.eps,
                                            op0=ALU.mult, op1=ALU.add), 1)
            v.wait_ge(self.ve.h, t)
            res["varb"] = self.ve.inc(
                v.tensor_sub(ln["varb"][:], ln["varb"][:], ln["msq"][:]), 1)
        self.V(v1)

        def s2(s):
            s.wait_ge(self.ve.h, res["varb"])
            res["stdb"] = self.sc.inc(
                s.activation(out=ln["stdb"][:], in_=ln["varb"][:], func=AF.Sqrt), 1)
        self.S(s2)

        def v3(v):
            v.wait_ge(self.sc.h, res["stdb"])
            t = self.ve.inc(v.reciprocal(out=ln["rstd"][:], in_=ln["stdb"][:]), 1)
            v.wait_ge(self.ve.h, t)
            t = self.ve.inc(v.scalar_tensor_tensor(
                out=ln["nmr"][:], in0=ln["m"][:], scalar=-1.0, in1=ln["rstd"][:],
                op0=ALU.mult, op1=ALU.mult), 1)
            v.wait_ge(self.ve.h, t)
            res["out"] = self.ve.inc(
                v.tensor_scalar(out=x2_out[:], in0=v_in[:],
                                scalar1=ln["rstd"][:], scalar2=ln["nmr"][:],
                                op0=ALU.mult, op1=ALU.add), 1)
        self.V(v3)
        return res["out"]

    def _ln_stats(self, v, ln, v_in):
        cfg = self.cfg
        t = self.ve.inc(v.reduce_sum(out=ln["srow"][:], in_=v_in[:], axis=AX.X), 1)
        t = self.ve.inc(v.scalar_tensor_tensor(
            out=ln["sqdump"][:], in0=v_in[:], scalar=1.0, in1=v_in[:],
            op0=ALU.mult, op1=ALU.mult, accum_out=ln["ssq"][:]), 1)
        v.wait_ge(self.ve.h, t)
        t = self.ve.inc(v.tensor_scalar_mul(ln["m"][:], ln["srow"][:], 1.0 / cfg.H), 1)
        v.wait_ge(self.ve.h, t)
        t = self.ve.inc(v.tensor_mul(ln["msq"][:], ln["m"][:], ln["m"][:]), 1)
        v.wait_ge(self.ve.h, t)
        t = self.ve.inc(v.tensor_scalar(out=ln["varb"][:], in0=ln["ssq"][:],
                                        scalar1=1.0 / cfg.H, scalar2=cfg.eps,
                                        op0=ALU.mult, op1=ALU.add), 1)
        v.wait_ge(self.ve.h, t)
        return self.ve.inc(v.tensor_sub(ln["varb"][:], ln["varb"][:], ln["msq"][:]), 1)

    def _ln_sqrt(self, s, ln, varb_t):
        s.wait_ge(self.ve.h, varb_t)
        return self.sc.inc(s.activation(out=ln["stdb"][:], in_=ln["varb"][:], func=AF.Sqrt), 1)

    def _ln_apply(self, v, ln, v_in, x2_out, stdb_t):
        v.wait_ge(self.sc.h, stdb_t)
        t = self.ve.inc(v.reciprocal(out=ln["rstd"][:], in_=ln["stdb"][:]), 1)
        v.wait_ge(self.ve.h, t)
        t = self.ve.inc(v.scalar_tensor_tensor(
            out=ln["nmr"][:], in0=ln["m"][:], scalar=-1.0, in1=ln["rstd"][:],
            op0=ALU.mult, op1=ALU.mult), 1)
        v.wait_ge(self.ve.h, t)
        return self.ve.inc(
            v.tensor_scalar(out=x2_out[:], in0=v_in[:],
                            scalar1=ln["rstd"][:], scalar2=ln["nmr"][:],
                            op0=ALU.mult, op1=ALU.add), 1)

    def _emit_p3a(self, ln, cload, xload, v_sb, x2buf, x2bbuf):
        cfg = self.cfg
        done = {}
        spills = {}
        for rp in range(0, cfg.RT, 2):
            pair = [rp] + ([rp + 1] if rp + 1 < cfg.RT else [])
            st = {}

            def y1(s, pair=pair):
                for rt in pair:
                    L = rt % 2
                    self.d_cx.begin(s)
                    if rt - 2 in done:
                        s.wait_ge(self.ve.h, done[rt - 2])
                    self.d_cx.dma(s, cload[L][:], self.C_d[rt])
                    st[(rt, "x")] = self.d_cx.dma(
                        s, xload[L][:], self.x_row[rt * 128:(rt + 1) * 128, :])
            self.Y(y1)

            def v1(v, pair=pair):
                for rt in pair:
                    L = rt % 2
                    v.wait_ge(self.d_cx.h, st[(rt, "x")])
                    if rt - 2 in done:
                        v.wait_ge(self.ve.h, done[rt - 2])
                    if rt - 2 in spills:
                        self.d_x2.wait(v, spills[rt - 2])
                    t = self.ve.inc(v.tensor_tensor(out=v_sb[L][:], in0=cload[L][:],
                                                    in1=xload[L][:], op=ALU.add), 1)
                    v.wait_ge(self.ve.h, t)
                    st[(rt, "var")] = self._ln_stats(v, ln[L], v_sb[L])
            self.V(v1)

            def s1(s, pair=pair):
                for rt in pair:
                    L = rt % 2
                    st[(rt, "std")] = self._ln_sqrt(s, ln[L], st[(rt, "var")])
            self.S(s1)

            def v2(v, pair=pair):
                for rt in pair:
                    L = rt % 2
                    done[rt] = self._ln_apply(v, ln[L], v_sb[L], x2bbuf[L], st[(rt, "std")])
            self.V(v2)

            def y2(s, pair=pair):
                for rt in pair:
                    L = rt % 2
                    self.d_x2.begin(s)
                    s.wait_ge(self.ve.h, done[rt])
                    self.d_x2.dma(s, self.x2_d[rt], x2bbuf[L][:])
                    dst = self.x2b_d[rt].rearrange("dc r p -> r dc p")
                    srcap = x2bbuf[L][:].rearrange("r (dc p) -> r dc p", dc=cfg.HC)
                    spills[rt] = self.d_x2.dma(s, dst, srcap)
            self.block.scalar(y2)

    def _emit_p3b(self, wf_sb, x2bT, scale_sb, bias_sb, y_sb, yb_sb, ps_y):
        cfg = self.cfg
        OQ = cfg.O // 4
        st0 = {}

        def y0(s):
            self.d_sb.begin(s)
            st0["sc"] = self.d_sb.dma(s, scale_sb[:], self.scale_b[:])
            st0["bi"] = self.d_sb.dma(s, bias_sb[:], self.bias_b[:])
        self.Y(y0)

        wf_ready = {}
        mm_done = {}
        y_done = {}
        xbt_lastuse = [0, 0]
        wf_lastuse = [0, 0]

        def load_wf(g, oq):
            sl = oq % 2
            self.d_wf.begin(g)
            if wf_lastuse[sl]:
                g.wait_ge(self.te.h, wf_lastuse[sl])
            wf_ready[oq] = self.d_wf.dma(g, wf_sb[sl][:], self.wf_t[oq])

        for oq in range(4):
            oqs = slice(oq * OQ, (oq + 1) * OQ)
            sl = oq % 2

            def gw(g, oq=oq):
                if oq == 0:
                    load_wf(g, 0)
                if oq + 1 < 4:
                    load_wf(g, oq + 1)
            self.G(gw)

            for rt in range(cfg.RT):
                par = rt % 2

                def yx(s, oq=oq, rt=rt, par=par):
                    self.d_xbt[par].begin(s)
                    if xbt_lastuse[par]:
                        s.wait_ge(self.te.h, xbt_lastuse[par])
                    src = self.x2b_d[rt].rearrange("dc r p -> (dc r) p")
                    st0[(oq, rt, "x")] = self.d_xbt[par].dma(s, x2bT[par][:], src,
                                                             transpose=True)
                self.Y(yx)

                def tm(t, oq=oq, rt=rt, par=par, sl=sl):
                    t.wait_ge(self.d_wf.h, wf_ready[oq])
                    t.wait_ge(self.d_xbt[par].h, st0[(oq, rt, "x")])
                    key = (oq, rt - 2) if rt >= 2 else (oq - 1, cfg.RT - 2 + rt) if oq > 0 else None
                    if key is not None and key in y_done:
                        t.wait_ge(self.ve.h, y_done[key])
                    last = None
                    for dc in range(cfg.HC):
                        for (c0, c1) in _chunks(OQ):
                            last = t.matmul(ps_y[par][:, c0:c1],
                                            x2bT[par][:, dc * 128:(dc + 1) * 128],
                                            wf_sb[sl][:, dc * OQ + c0: dc * OQ + c1],
                                            start=(dc == 0), stop=(dc == cfg.HC - 1))
                    mm_done[(oq, rt)] = self.te.inc(last, 1)
                    xbt_lastuse[par] = mm_done[(oq, rt)]
                    wf_lastuse[sl] = mm_done[(oq, rt)]
                self.T(tm)

                def vy(v, oq=oq, rt=rt, par=par, oqs=oqs):
                    v.wait_ge(self.te.h, mm_done[(oq, rt)])
                    v.wait_ge(self.d_sb.h, st0["bi"])
                    self.d_ys[par].wait(v)
                    t = self.ve.inc(v.tensor_tensor(out=y_sb[par][:], in0=ps_y[par][:, 0:OQ],
                                                    in1=scale_sb[:, oqs], op=ALU.mult), 1)
                    v.wait_ge(self.ve.h, t)
                    y_done[(oq, rt)] = self.ve.inc(
                        v.tensor_tensor(out=yb_sb[par][:], in0=y_sb[par][:],
                                        in1=bias_sb[:, oqs], op=ALU.add), 1)
                self.V(vy)

                def yy(s, oq=oq, rt=rt, par=par, oqs=oqs):
                    self.d_ys[par].begin(s)
                    s.wait_ge(self.ve.h, y_done[(oq, rt)])
                    self.d_ys[par].dma(s, self.y_d[rt, :, oqs], yb_sb[par][:])
                self.block.scalar(yy)

    def _emit_p3c(self, ln, yload, x2load, v_sb, o_sb):
        cfg = self.cfg
        done = {}
        outs = {}
        for rp in range(0, cfg.RT, 2):
            pair = [rp] + ([rp + 1] if rp + 1 < cfg.RT else [])
            st = {}

            def y1(s, pair=pair):
                for rt in pair:
                    L = rt % 2
                    self.d_yx2.begin(s)
                    if rt - 2 in done:
                        s.wait_ge(self.ve.h, done[rt - 2])
                    self.d_yx2.dma(s, yload[L][:], self.y_d[rt])
                    st[(rt, "x2")] = self.d_yx2.dma(s, x2load[L][:], self.x2_d[rt])
            self.Y(y1)

            def v1(v, pair=pair):
                for rt in pair:
                    L = rt % 2
                    v.wait_ge(self.d_yx2.h, st[(rt, "x2")])
                    if rt - 2 in done:
                        v.wait_ge(self.ve.h, done[rt - 2])
                    if rt - 2 in outs:
                        self.dout.wait(v, outs[rt - 2])
                    t = self.ve.inc(v.tensor_tensor(out=v_sb[L][:], in0=yload[L][:],
                                                    in1=x2load[L][:], op=ALU.add), 1)
                    v.wait_ge(self.ve.h, t)
                    st[(rt, "var")] = self._ln_stats(v, ln[L], v_sb[L])
            self.V(v1)

            def s1(s, pair=pair):
                for rt in pair:
                    L = rt % 2
                    st[(rt, "std")] = self._ln_sqrt(s, ln[L], st[(rt, "var")])
            self.S(s1)

            def v2(v, pair=pair):
                for rt in pair:
                    L = rt % 2
                    done[rt] = self._ln_apply(v, ln[L], v_sb[L], o_sb[L], st[(rt, "std")])
            self.V(v2)

            def y2(s, pair=pair):
                for rt in pair:
                    L = rt % 2
                    self.dout.begin(s)
                    s.wait_ge(self.ve.h, done[rt])
                    outs[rt] = self.dout.dma(s, self.out_d[rt * 128:(rt + 1) * 128, :],
                                             o_sb[L][:])
            self.block.scalar(y2)


def _prep_core_inputs(x_c, host):
    rows, H = x_c.shape
    HC = H // 128
    x_t = np.ascontiguousarray(
        x_c.reshape(2, rows // 2, HC, 128).transpose(0, 3, 2, 1).reshape(2, 128, HC * rows // 2))
    d = {"x_t": x_t, "x_row": np.ascontiguousarray(x_c)}
    d.update(host)
    return d


def _prep_shared(wg, wu, wd, wf, scale, bias, cfg: Cfg):
    I, H, O = cfg.I, cfg.H, cfg.O
    IT, HC, NB, NBW, OH = cfg.IT, cfg.HC, cfg.NB, cfg.NBW, cfg.OH
    wg_t = np.ascontiguousarray(
        wg.reshape(IT, 128, HC, 128).transpose(0, 3, 2, 1).reshape(IT, 128, HC * 128))
    wu_t = np.ascontiguousarray(
        wu.reshape(IT, 128, HC, 128).transpose(0, 3, 2, 1).reshape(IT, 128, HC * 128))
    wd_t = np.ascontiguousarray(
        wd.reshape(NB, NBW, IT, 128).transpose(0, 3, 2, 1))
    OQ = O // 4
    wf_t = np.ascontiguousarray(
        wf.reshape(4, OQ, HC, 128).transpose(0, 3, 2, 1).reshape(4, 128, HC * OQ))
    scale_b = np.ascontiguousarray(np.broadcast_to(scale.reshape(1, O), (128, O)).astype(np.float32))
    bias_b = np.ascontiguousarray(np.broadcast_to(bias.reshape(1, O), (128, O)).astype(np.float32))
    return {"wg_t": wg_t, "wu_t": wu_t, "wd_t": wd_t, "wf_t": wf_t,
            "scale_b": scale_b, "bias_b": bias_b}


def kernel(L0_attn_x, L0_attn_w_gate, L0_attn_w_up, L0_attn_w_down,
           L0_ffn_weight_int8, L0_ffn_weight_scale, L0_ffn_bias):
    cfg = Cfg()
    x = np.ascontiguousarray(np.asarray(L0_attn_x, dtype=np.float32)).reshape(-1, cfg.H)
    wg = np.ascontiguousarray(np.asarray(L0_attn_w_gate, dtype=np.float32))
    wu = np.ascontiguousarray(np.asarray(L0_attn_w_up, dtype=np.float32))
    wd = np.ascontiguousarray(np.asarray(L0_attn_w_down, dtype=np.float32))
    wf = np.ascontiguousarray(np.asarray(L0_ffn_weight_int8, dtype=np.int8))
    scale = np.asarray(L0_ffn_weight_scale, dtype=np.float32)
    bias = np.asarray(L0_ffn_bias, dtype=np.float32)

    shared = _prep_shared(wg, wu, wd, wf, scale, bias, cfg)
    in_maps = []
    for c in range(N_CORES):
        x_c = x[c * cfg.rows:(c + 1) * cfg.rows]
        in_maps.append(_prep_core_inputs(x_c, shared))

    nc = Builder(cfg).build()
    res = run_bass_kernel_spmd(nc, in_maps, core_ids=list(range(N_CORES)))
    out = np.concatenate([res.results[c]["out"] for c in range(N_CORES)], axis=0)
    B, S = np.asarray(L0_attn_x).shape[:2]
    return out.reshape(B, S, cfg.H).astype(np.float32)
